# version 10
# baseline (speedup 1.0000x reference)
import math
import numpy as np
import concourse.bass as bass
import concourse.mybir as mybir
from concourse.bass_utils import run_bass_kernel_spmd

F32 = mybir.dt.float32
BF16 = mybir.dt.bfloat16
AF = mybir.ActivationFunctionType
ALU = mybir.AluOpType

ENGINES = ("tensor", "vector", "scalar", "sync", "gpsimd")

T_CORE = 1024
NHALF = 2
NT = 512
HC = 32
SC = 8
FC_P = 26
D = 4096
S_DIM = 1024
F_DIM = 9984
EPS = 1e-5
THD = 2.5

_DEBUG_DUMP = ()
_CUT = []


class Sem:
    __slots__ = ("h", "n", "name")

    def __init__(self, h, name):
        self.h = h
        self.n = 0
        self.name = name


class Prog:
    def __init__(self, nc):
        self.nc = nc
        self.q = {e: [] for e in ENGINES}
        self.sems = []
        self.cuts = []

    def sem(self, name):
        s = Sem(self.nc.alloc_semaphore(name), name)
        self.sems.append(s)
        return s

    def op(self, eng, fn, waits=(), inc=None, inc16=None):
        incs = []
        if inc is not None:
            inc.n += 1
            incs.append((inc, 1))
        if inc16 is not None:
            inc16.n += 16
            incs.append((inc16, 16))
        self.q[eng].append((tuple(waits), fn, tuple(incs)))

    def dma(self, eng, out_ap, in_ap, waits, sem):
        def fn(e, o=out_ap, i=in_ap):
            return e.dma_start(o, i)
        self.op(eng, fn, waits, inc16=sem)

    def barrier(self, sbar, dma_sems, eng_sems=()):
        eng_sems = dict(eng_sems)
        dma_waits = [(s, s.n) for s in dma_sems]
        for e in ENGINES:
            if e == "gpsimd":
                continue
            w = list(dma_waits) if e == "sync" else []
            if e in eng_sems:
                s = eng_sems[e]
                w.append((s, s.n))
            self.op(e, lambda eng: eng.nop(), w, inc=sbar)
        tgt = sbar.n
        for e in ENGINES:
            if e == "gpsimd":
                continue
            self.op(e, lambda eng: eng.nop(), [(sbar, tgt)])
        self.cuts.append({e: len(self.q[e]) for e in ENGINES})

    def emit(self, cut=None):
        if cut is not None:
            lim = self.cuts[cut]
            self.q = {e: self.q[e][:lim[e]] for e in ENGINES}
        def runner(items):
            def run(eng):
                for waits, fn, incs in items:
                    for (sem, val) in waits:
                        eng.wait_ge(sem.h, val)
                    inst = fn(eng)
                    for (sem, k) in incs:
                        inst = inst.then_inc(sem.h, k)
            return run

        with self.nc.Block() as block:
            block.tensor(runner(self.q["tensor"]))
            block.vector(runner(self.q["vector"]))
            block.scalar(runner(self.q["scalar"]))
            block.sync(runner(self.q["sync"]))
            block.gpsimd(runner(self.q["gpsimd"]))


class StatsBank:

    def __init__(self, ap, n_banks=2):
        self.ap = ap
        self.touched = [False] * n_banks

    def reset(self):
        for i in range(len(self.touched)):
            self.touched[i] = False

    def start_flag(self, col0, col1):
        b = col0 // 512
        assert col1 <= (b + 1) * 512
        st = not self.touched[b]
        self.touched[b] = True
        return st


def build_core_program(debug_dump=()):
    nc = bass.Bass("TRN2", target_bir_lowering=False)

    def din(name, shape):
        return nc.dram_tensor(name, shape, F32, kind="ExternalInput").ap()

    x_in = din("x_in", [T_CORE, D])
    w_up_p = din("w_up_p", [SC, 128, HC * 128])
    w_gate_p = din("w_gate_p", [SC, 128, HC * 128])
    lp_pc = din("lp_pc", [128, SC])
    df_bc = din("df_bc", [128, 6])
    w_cat_p = din("w_cat_p", [HC, 128, 56 * 128])
    w_fg_p = din("w_fg_p", [78, 128, HC * 128])
    w_fu_p = din("w_fu_p", [78, 128, HC * 128])
    w_fd_p = din("w_fd_p", [HC, 3, 128, FC_P * 128])
    w_b_p = din("w_b_p", [HC, 128, HC * 128])
    bb_pc = din("bb_pc", [128, HC])
    w_rT = din("w_rT", [D, 8])
    a_T = din("a_T", [D, 128])
    b_fl = din("b_fl", [128, D])

    out_hbm = nc.dram_tensor("out", [T_CORE, D], F32, kind="ExternalOutput").ap()

    def scratch(name, shape, dtype=F32):
        kind = "ExternalOutput" if name in debug_dump else "Internal"
        return nc.dram_tensor(name, shape, dtype, kind=kind).ap()

    sc_tmp = scratch("sc_tmp", [7, SC, 128, T_CORE], BF16)
    sc_a1A = scratch("sc_a1A", [HC, 128, T_CORE])
    sc_a1B = scratch("sc_a1B", [HC, 128, T_CORE])
    sc_a1 = scratch("sc_a1", [HC, 128, T_CORE])
    sc_a = scratch("sc_a", [8, 128, D])
    sc_x2 = scratch("sc_x2", [8, 128, D])
    sc_f = scratch("sc_f", [8, 128, D])

    ident_d = nc.inline_tensor(np.eye(128, dtype=np.float32), "ident")
    import ml_dtypes
    identb_d = nc.inline_tensor(np.eye(128).astype(ml_dtypes.bfloat16), "identb")
    ones_row_d = nc.inline_tensor(np.ones((1, 128), np.float32), "ones_row")
    neg_row_d = nc.inline_tensor(np.full((1, 128), -1.0, np.float32), "neg_row")
    ones_col_d = nc.inline_tensor(np.ones((128, 1), np.float32), "ones_col")
    esel = np.zeros((8, 128), np.float32)
    for e in range(8):
        esel[e, 16 * e:16 * (e + 1)] = 2.0
    esel_d = nc.inline_tensor(esel, "esel")

    XL = nc.alloc_sbuf_tensor("XL", [128, HC * T_CORE], BF16, side="left")
    ident = nc.alloc_sbuf_tensor("ident_sb", [128, 128], F32, side="left")
    identb = nc.alloc_sbuf_tensor("identb_sb", [128, 128], BF16, side="left")
    lncol = nc.alloc_sbuf_tensor("lncol", [128, 48], F32, side="left")
    ones_row = nc.alloc_sbuf_tensor("ones_row_sb", [1, 128], F32, side="left")
    neg_row = nc.alloc_sbuf_tensor("neg_row_sb", [1, 128], F32, side="left")
    ones_col = nc.alloc_sbuf_tensor("ones_col_sb", [128, 1], F32, side="left")
    esel_sb = nc.alloc_sbuf_tensor("esel_sb", [8, 128], F32, side="left")
    lp_sb = nc.alloc_sbuf_tensor("lp_sb", [128, SC], F32, side="left")
    ndf_sb = nc.alloc_sbuf_tensor("ndf_sb", [128, 6], F32, side="left")
    bb_sb = nc.alloc_sbuf_tensor("bb_sb", [128, HC], F32, side="left")
    r_mean = nc.alloc_sbuf_tensor("r_mean", [1, T_CORE], F32, side="left")
    r_tmp = nc.alloc_sbuf_tensor("r_tmp", [1, T_CORE], F32, side="left")
    r_aux = nc.alloc_sbuf_tensor("r_aux", [1, T_CORE], F32, side="left")
    r_mb = nc.alloc_sbuf_tensor("r_mb", [1, T_CORE], F32, side="left")
    bc_t = nc.alloc_sbuf_tensor("bc_t", [128, 512], F32, side="left")

    Pb = [nc.alloc_psum_tensor("P%d" % i, [128, 512], F32) for i in range(4)]
    S0 = nc.alloc_psum_tensor("S0", [1, T_CORE], F32)
    S1 = nc.alloc_psum_tensor("S1", [1, T_CORE], F32)
    st0 = StatsBank(S0)
    st1 = StatsBank(S1)

    P = Prog(nc)
    s_t = P.sem("s_t")
    s_v = P.sem("s_v")
    s_s = P.sem("s_s")
    s_bar = P.sem("s_bar")
    s_w = [P.sem("s_w0"), P.sem("s_w1")]
    s_i = [P.sem("s_i0"), P.sem("s_i1")]
    s_j = [P.sem("s_j0"), P.sem("s_j1")]
    s_o = [P.sem("s_o0"), P.sem("s_o1")]
    s_k = P.sem("s_k")
    s_k2 = P.sem("s_k2")
    DMA_SEMS = tuple(s_i + s_j + s_o + [s_k])
    ALL_DMA_SEMS = tuple(s_w + s_i + s_j + s_o + [s_k, s_k2])

    bank_free = {i: [] for i in range(4)}

    def TOP(fn, waits=(), inc=None):
        P.op("tensor", fn, waits, inc=inc)

    def mm_group(psum_ap, pairs, waits, bank, extra_first_start=True):
        w = list(waits) + bank_free[bank]

        def fn(eng, pairs=pairs, out=psum_ap):
            inst = None
            n = len(pairs)
            for i, (l, r) in enumerate(pairs):
                inst = eng.matmul(out, l, r,
                                  start=(extra_first_start and i == 0),
                                  stop=(i == n - 1), skip_group_check=True)
            return inst
        P.op("tensor", fn, w, inc=s_t)
        return (s_t, s_t.n)

    def stats_mms(entries, waits):
        plan = []
        for (sb, c0, c1, rhs) in entries:
            st = sb.start_flag(c0, c1)
            plan.append((sb.ap[0:1, c0:c1], rhs, st))

        def fn(eng, plan=plan, oc=ones_col):
            inst = None
            for (outap, rhs, st) in plan:
                inst = eng.matmul(outap, oc[:, :], rhs, start=st, stop=True,
                                  skip_group_check=True)
            return inst
        P.op("tensor", fn, waits, inc=s_t)
        return (s_t, s_t.n)

    def V(fn, waits=(), inc=True):
        w = list(waits) + [(s_v, s_v.n)]
        P.op("vector", fn, w, inc=s_v if inc else None)
        return (s_v, s_v.n)

    def S(fn, waits=(), inc=True):
        w = list(waits) + [(s_s, s_s.n)]
        P.op("scalar", fn, w, inc=s_s if inc else None)
        return (s_s, s_s.n)

    def ln_rows(col0, col1, mm_tok, denom):
        c0, c1 = col0, col1
        V(lambda e: e.tensor_scalar(r_mean[:, c0:c1], S0[0:1, c0:c1],
                                    1.0 / denom, None, ALU.mult), [mm_tok])
        V(lambda e: e.tensor_scalar(r_tmp[:, c0:c1], S1[0:1, c0:c1],
                                    1.0 / denom, None, ALU.mult))
        stats_free = (s_v, s_v.n)
        V(lambda e: e.tensor_tensor(r_aux[:, c0:c1], r_mean[:, c0:c1],
                                    r_mean[:, c0:c1], ALU.mult))
        V(lambda e: e.scalar_tensor_tensor(r_mb[:, c0:c1], r_tmp[:, c0:c1],
                                           EPS, r_aux[:, c0:c1], ALU.add,
                                           ALU.subtract))
        V(lambda e: e.reciprocal(r_aux[:, c0:c1], r_mb[:, c0:c1]))
        tv = (s_v, s_v.n)
        ts = S(lambda e: e.activation(r_tmp[:, c0:c1], r_aux[:, c0:c1],
                                      AF.Sqrt), [tv])
        tmb = V(lambda e: e.tensor_tensor(r_mb[:, c0:c1], r_mean[:, c0:c1],
                                          r_tmp[:, c0:c1], ALU.mult), [ts])
        toks = []
        for cc in range(c0, c1, 512):
            bidx = (cc // 512) % 2
            t1 = mm_group(Pb[2 + bidx][:, :], [(ones_row[:, :], r_tmp[:, cc:cc + 512])],
                          [tmb], 2 + bidx)
            tv1 = V(lambda e, cc=cc, b=bidx: e.tensor_copy(bc_r[:, cc:cc + 512],
                                                           Pb[2 + b][:, :]), [t1])
            bank_free[2 + bidx] = [tv1]
            t2 = mm_group(Pb[2 + bidx][:, :], [(neg_row[:, :], r_mb[:, cc:cc + 512])],
                          [tmb], 2 + bidx)
            tv2 = V(lambda e, cc=cc, b=bidx: e.tensor_copy(bc_mb[:, cc:cc + 512],
                                                           Pb[2 + b][:, :]), [t2])
            bank_free[2 + bidx] = [tv2]
            toks.append(tv2)
        return toks[-1], stats_free

    def ln_cols(dst_base, mm_tok, denom):
        c0, c1 = 0, T_CORE
        V(lambda e: e.tensor_scalar(r_mean[:, c0:c1], S0[0:1, c0:c1],
                                    1.0 / denom, None, ALU.mult), [mm_tok])
        V(lambda e: e.tensor_scalar(r_tmp[:, c0:c1], S1[0:1, c0:c1],
                                    1.0 / denom, None, ALU.mult))
        V(lambda e: e.tensor_tensor(r_aux[:, c0:c1], r_mean[:, c0:c1],
                                    r_mean[:, c0:c1], ALU.mult))
        V(lambda e: e.scalar_tensor_tensor(r_mb[:, c0:c1], r_tmp[:, c0:c1],
                                           EPS, r_aux[:, c0:c1], ALU.add,
                                           ALU.subtract))
        V(lambda e: e.reciprocal(r_aux[:, c0:c1], r_mb[:, c0:c1]))
        tv = (s_v, s_v.n)
        ts = S(lambda e: e.activation(r_tmp[:, c0:c1], r_aux[:, c0:c1],
                                      AF.Sqrt), [tv])
        tmb = V(lambda e: e.scalar_tensor_tensor(
            r_mb[:, c0:c1], r_mean[:, c0:c1], -1.0, r_tmp[:, c0:c1],
            ALU.mult, ALU.mult), [ts])
        def tpc(eng):
            inst = None
            for tt in range(8):
                inst = eng.transpose(Pb[2][:, tt:tt + 1],
                                     r_tmp[0:1, tt * 128:(tt + 1) * 128],
                                     ident[0:1, 0:1])
                inst = eng.transpose(Pb[2][:, 8 + tt:9 + tt],
                                     r_mb[0:1, tt * 128:(tt + 1) * 128],
                                     ident[0:1, 0:1])
            return inst
        P.op("tensor", tpc, [tmb] + bank_free[2], inc=s_t)
        t_c = (s_t, s_t.n)
        evc = V(lambda e, dst_base=dst_base: e.tensor_copy(
            lncol[:, dst_base:dst_base + 16], Pb[2][:, 0:16]), [t_c])
        bank_free[2] = [evc]
        return evc

    def xl(hc, n=None):
        if n is None:
            return XL[:, hc * T_CORE:(hc + 1) * T_CORE]
        return XL[:, hc * T_CORE + n * NT: hc * T_CORE + (n + 1) * NT]

    import contextlib
    stack = contextlib.ExitStack()
    w_stack = contextlib.ExitStack()

    _ph = {}

    def ph_open():
        es = contextlib.ExitStack()
        _ph["es"] = es
        return es

    def sbr(name, shape, dtype=F32):
        return _ph["es"].enter_context(
            nc.sbuf_tensor(name, shape, dtype, side="right"))

    def ph_close():
        _ph["es"].close()

    W_glob = w_stack.enter_context(
        nc.sbuf_tensor("W_glob", [128, 16384], BF16, side="right"))
    p_stack = contextlib.ExitStack()
    asb8 = p_stack.enter_context(
        nc.sbuf_tensor("asb8", [128, HC * 128], BF16, side="right"))
    wr8 = p_stack.enter_context(
        nc.sbuf_tensor("wr8", [128, HC * 8], BF16, side="right"))
    bfl8 = p_stack.enter_context(
        nc.sbuf_tensor("bfl8", [128, D], BF16, side="right"))

    P.dma("sync", ident[:, :], ident_d.ap(), [], s_k)
    P.dma("sync", identb[:, :], identb_d.ap(), [], s_k)
    P.dma("sync", ones_row[:, :], ones_row_d.ap(), [], s_k)
    P.dma("sync", neg_row[:, :], neg_row_d.ap(), [], s_k)
    P.dma("sync", ones_col[:, :], ones_col_d.ap(), [], s_k)
    P.dma("sync", esel_sb[:, :], esel_d.ap(), [], s_k)
    P.dma("sync", lp_sb[:, :], lp_pc, [], s_k)
    P.dma("sync", bb_sb[:, :], bb_pc, [], s_k)
    ndf_raw = nc.alloc_sbuf_tensor("ndf_raw", [128, 6], F32, side="left")
    P.dma("sync", ndf_raw[:, :], df_bc, [], s_k)
    tload = (s_k, s_k.n)
    V(lambda e: e.tensor_scalar(ndf_sb[:, :], ndf_raw[:, :], -1.0, None,
                                ALU.mult), [tload])
    P.barrier(s_bar, DMA_SEMS, {"tensor": s_t, "vector": s_v, "scalar": s_s})

    ph_open()
    xraw = sbr("xraw", [128, 2 * D], F32)
    junk1 = sbr("junk1", [128, D], F32)
    stc = sbr("stc", [128, 20], F32)
    tn1 = sbr("tn1", [128, 512], F32)
    xraw_free = [[], []]
    stc_free = [[], []]
    bc_free = [[], []]
    for tt in range(8):
        b = tt % 2
        o = 9 * b
        P.dma("sync", xraw[:, b * D:(b + 1) * D],
              x_in[tt * 128:(tt + 1) * 128, :], xraw_free[b], s_i[b])
        ld = (s_i[b], s_i[b].n)
        tsum = V(lambda e, b=b, o=o: e.reduce_sum(
            stc[:, o + 0:o + 1], xraw[:, b * D:(b + 1) * D], mybir.AxisListType.X),
            [ld])
        tsq = S(lambda e, b=b, o=o: e.activation(
            junk1[:, :], xraw[:, b * D:(b + 1) * D], AF.Square,
            accum_out=stc[:, o + 1:o + 2]), [ld] + stc_free[b])
        V(lambda e, o=o: e.tensor_scalar(stc[:, o + 2:o + 3], stc[:, o + 0:o + 1], 1.0 / D, None,
                                    ALU.mult), [tsum])
        V(lambda e, o=o: e.tensor_scalar(stc[:, o + 3:o + 4], stc[:, o + 1:o + 2], 1.0 / D, None,
                                    ALU.mult), [tsq])
        V(lambda e, o=o: e.tensor_tensor(stc[:, o + 4:o + 5], stc[:, o + 2:o + 3], stc[:, o + 2:o + 3],
                                    ALU.mult))
        V(lambda e, o=o: e.scalar_tensor_tensor(stc[:, o + 5:o + 6], stc[:, o + 3:o + 4], EPS,
                                           stc[:, o + 4:o + 5], ALU.add, ALU.subtract))
        tv = V(lambda e, o=o: e.reciprocal(stc[:, o + 6:o + 7], stc[:, o + 5:o + 6]))
        ts = S(lambda e, o=o: e.activation(stc[:, o + 7:o + 8], stc[:, o + 6:o + 7], AF.Sqrt), [tv])
        tmb = V(lambda e, o=o: e.tensor_tensor(stc[:, o + 8:o + 9], stc[:, o + 2:o + 3],
                                          stc[:, o + 7:o + 8], ALU.mult), [ts])
        def tpr(eng, o=o):
            eng.transpose(Pb[2][0:1, 0:128], stc[:, o + 7:o + 8], ident[:, :])
            return eng.transpose(Pb[2][0:1, 128:256], stc[:, o + 8:o + 9], ident[:, :])
        P.op("tensor", tpr, [tmb] + bank_free[2], inc=s_t)
        t_rows = (s_t, s_t.n)
        ev_rows = V(lambda e: e.tensor_copy(r_tmp[0:1, 0:256], Pb[2][0:1, 0:256]),
                    [t_rows])
        bank_free[2] = [ev_rows]
        def bcast(eng):
            eng.matmul(Pb[3][:, 0:128], ones_row[:, :], r_tmp[0:1, 0:128],
                       start=True, stop=True, skip_group_check=True)
            return eng.matmul(Pb[3][:, 128:256], neg_row[:, :],
                              r_tmp[0:1, 128:256], start=True, stop=True,
                              skip_group_check=True)
        P.op("tensor", bcast, [ev_rows] + bank_free[3] + bc_free[b], inc=s_t)
        t_bc = (s_t, s_t.n)
        ev_bc = V(lambda e, b=b: e.tensor_copy(
            bc_t[:, b * 256:b * 256 + 128], Pb[3][:, 0:128]), [t_bc])
        ev_bc2 = V(lambda e, b=b: e.tensor_copy(
            bc_t[:, b * 256 + 128:b * 256 + 256], Pb[3][:, 128:256]))
        bank_free[3] = [ev_bc2]
        last_tp = None
        for g in range(8):
            pbi = g % 2
            def tp_fn(eng, b=b, g=g, pbi=pbi):
                inst = None
                for jj in range(4):
                    inst = eng.transpose(Pb[pbi][:, jj * 128:(jj + 1) * 128],
                                         xraw[:, b * D + (4 * g + jj) * 128:
                                              b * D + (4 * g + jj + 1) * 128],
                                         ident[:, :])
                return inst
            P.op("tensor", tp_fn, [ld, tsq] + bank_free[pbi], inc=s_t)
            tp = (s_t, s_t.n)
            last_tp = tp
            rrep = bc_t[:, b * 256:b * 256 + 128].unsqueeze(1).broadcast_to(
                [128, 4, 128])
            mrep = bc_t[:, b * 256 + 128:b * 256 + 256].unsqueeze(1).broadcast_to(
                [128, 4, 128])
            t1 = V(lambda e, pbi=pbi, rrep=rrep: e.tensor_tensor(
                tn1[:, :].rearrange("p (c t) -> p c t", c=4),
                Pb[pbi][:, :].rearrange("p (c t) -> p c t", c=4), rrep,
                ALU.mult), [tp, ev_bc2])
            bank_free[pbi] = [t1]
            n = tt // 4
            j = tt % 4
            dst = XL[:, :].rearrange("p (c t) -> p c t", c=HC)[
                :, 4 * g:4 * g + 4, n * NT + j * 128: n * NT + (j + 1) * 128]
            t_xl = V(lambda e, dst=dst, mrep=mrep: e.tensor_tensor(
                dst, tn1[:, :].rearrange("p (c t) -> p c t", c=4), mrep,
                ALU.add))
        xraw_free[b] = [last_tp, tsq, tsum]
        bc_free[b] = [t_xl]
        stc_free[b] = [tmb]
        if tt == 3:
            xl_ready_h0 = [t_xl]
        if tt == 7:
            xl_ready_h1 = [t_xl]
    xl_ready = [xl_ready_h0, xl_ready_h1]

    ph_close()
    if "xl1" in debug_dump:
        dbg_xl = nc.dram_tensor("dbg_xl1", [128, HC * T_CORE], BF16,
                                kind="ExternalOutput").ap()
        P.dma("sync", dbg_xl, XL[:, :], xl_ready[0] + xl_ready[1], s_k)
        P.barrier(s_bar, DMA_SEMS, {"tensor": s_t, "vector": s_v, "scalar": s_s})

    ph_open()
    WB2 = W_glob
    tmpr = sbr("tmpr", [128, 2 * 7 * T_CORE], BF16)
    au_sb = sbr("au_sb", [128, 512], F32)
    aud_sb = sbr("aud_sb", [128, 512], F32)
    delta_sb = sbr("delta_sb", [128, 512], F32)
    absd_sb = sbr("absd_sb", [128, 512], F32)
    logd_sb = sbr("logd_sb", [128, 512], F32)
    keep_sb = sbr("keep_sb", [128, 512], F32)
    dt_sb = sbr("dt_sb", [128, 512], F32)
    dtk_sb = sbr("dtk_sb", [128, 512], F32)
    wb_free = [[], []]
    tmpr_free = [[], []]
    scratch_free = []
    for s in range(SC):
        wb = s % 2
        P.dma("gpsimd", WB2[:, wb * 8192: wb * 8192 + 4096], w_up_p[s],
              wb_free[wb], s_w[wb])
        P.dma("gpsimd", WB2[:, wb * 8192 + 4096: wb * 8192 + 8192],
              w_gate_p[s], [], s_w[wb])
        wtok = (s_w[wb], s_w[wb].n)
        mm_toks = []
        for n in range(NHALF):
            pairs_u = [(WB2[:, wb * 8192 + hc * 128: wb * 8192 + (hc + 1) * 128],
                        xl(hc, n)) for hc in range(HC)]
            t_au = mm_group(Pb[n][:, :], pairs_u, [wtok] + xl_ready[n], n)
            pairs_g = [(WB2[:, wb * 8192 + 4096 + hc * 128:
                            wb * 8192 + 4096 + (hc + 1) * 128],
                        xl(hc, n)) for hc in range(HC)]
            t_gp = mm_group(Pb[2 + n][:, :], pairs_g, [wtok] + xl_ready[n],
                            2 + n)
            mm_toks.append((t_au, t_gp))
        wb_free[wb] = [mm_toks[1][0], mm_toks[1][1]]
        for n in range(NHALF):
            t_au, t_gp = mm_toks[n]
            col = lambda m, wb=wb, n=n: wb * 7 * T_CORE + m * T_CORE + n * NT
            tv_au = V(lambda e, n=n: e.tensor_copy(au_sb[:, :], Pb[n][:, :]),
                      [t_au] + scratch_free + tmpr_free[wb])
            tv_aubf = V(lambda e, c=col(0), n=n: e.tensor_copy(
                tmpr[:, c:c + NT], Pb[n][:, :]))
            tv_delta = V(lambda e, s=s, n=n: e.tensor_scalar(
                delta_sb[:, :], Pb[2 + n][:, :], lp_sb[:, s:s + 1], None,
                ALU.subtract), [t_gp])
            bank_free[n] = [tv_aubf]
            bank_free[2 + n] = [tv_delta]
            ts_abs = S(lambda e: e.activation(absd_sb[:, :], delta_sb[:, :],
                                              AF.Abs), [tv_delta])
            ts_log = S(lambda e: e.activation(logd_sb[:, :], absd_sb[:, :],
                                              AF.Ln))
            tv_aud = V(lambda e: e.tensor_tensor(aud_sb[:, :], delta_sb[:, :],
                                                 au_sb[:, :], ALU.mult))
            tv_keep = V(lambda e: e.tensor_scalar(keep_sb[:, :], absd_sb[:, :],
                                                  THD, None, ALU.is_le),
                        [ts_abs])
            last_v = tv_keep
            for m in range(1, 7):
                odd = (m % 2 == 1)
                scl = float(m - 1) if odd else float(m)
                ts_dt = S(lambda e, scl=scl, m=m: e.activation(
                    dt_sb[:, :], logd_sb[:, :], AF.Exp,
                    bias=ndf_sb[:, m - 1:m], scale=scl),
                    [ts_log, last_v])
                src_dt = dt_sb
                tok_dt = ts_dt
                if m >= 5:
                    tok_dt = V(lambda e: e.tensor_tensor(
                        dtk_sb[:, :], dt_sb[:, :], keep_sb[:, :], ALU.mult),
                        [ts_dt, tv_keep])
                    src_dt = dtk_sb
                other = aud_sb if odd else au_sb
                last_v = V(lambda e, c=col(m), sd=src_dt, o=other:
                           e.tensor_tensor(tmpr[:, c:c + NT], sd[:, :],
                                           o[:, :], ALU.mult),
                           [tok_dt, tv_aud])
            scratch_free = [last_v]
        dst = sc_tmp[:, s].rearrange("m p t -> p m t")
        src = tmpr[:, wb * 7 * T_CORE:(wb + 1) * 7 * T_CORE].rearrange(
            "p (m t) -> p m t", m=7)
        P.dma("sync", dst, src, [scratch_free[0]], s_o[wb])
        tmpr_free[wb] = [(s_o[wb], s_o[wb].n)]
    tmp_spill_done = [(s_o[0], s_o[0].n), (s_o[1], s_o[1].n)]
    wbf_carry = wb_free[0] + wb_free[1]
    P.barrier(s_bar, DMA_SEMS, {"tensor": s_t, "vector": s_v, "scalar": s_s})

    ph_close()
    BIG = stack.enter_context(
        nc.sbuf_tensor("BIG", [128, 28 * T_CORE], BF16, side="right"))

    ph_open()
    WB3 = W_glob
    silu_sb = sbr("silu_sb", [128, 2 * 512], F32)
    a1in = sbr("a1in", [128, T_CORE], F32)
    a1out = sbr("a1out", [128, 2 * T_CORE], F32)
    part_in = [None, sc_a1A, sc_a1B]
    part_out = [sc_a1A, sc_a1B, sc_a1]
    for part in range(3):
        fbase = part * FC_P
        wb_free3 = [list(wbf_carry), list(wbf_carry)]
        silu_free = [[], [], [], []]
        big_toks = []
        for f in range(FC_P):
            wb = f % 2
            fa = fbase + f
            P.dma("gpsimd", WB3[:, wb * 8192: wb * 8192 + 4096],
                  w_fg_p[fa], wb_free3[wb], s_w[wb])
            P.dma("gpsimd", WB3[:, wb * 8192 + 4096: wb * 8192 + 8192],
                  w_fu_p[fa], [], s_w[wb])
            wtok = (s_w[wb], s_w[wb].n)
            mm_t = []
            for n in range(NHALF):
                pg = [(WB3[:, wb * 8192 + hc * 128: wb * 8192 + (hc + 1) * 128],
                       xl(hc, n)) for hc in range(HC)]
                tg = mm_group(Pb[n][:, :], pg, [wtok], n)
                pu = [(WB3[:, wb * 8192 + 4096 + hc * 128:
                           wb * 8192 + 4096 + (hc + 1) * 128],
                       xl(hc, n)) for hc in range(HC)]
                tu = mm_group(Pb[2 + n][:, :], pu, [wtok], 2 + n)
                mm_t.append((tg, tu))
            wb_free3[wb] = [mm_t[1][0], mm_t[1][1]]
            for n in range(NHALF):
                tg, tu = mm_t[n]
                sb = n
                ts_si = S(lambda e, sb=sb, n=n: e.activation(
                    silu_sb[:, sb * 512:(sb + 1) * 512], Pb[n][:, :],
                    AF.Silu), [tg] + silu_free[sb])
                bank_free[n] = [ts_si]
                col = f * T_CORE + n * NT
                tv = V(lambda e, c=col, sb=sb, n=n: e.tensor_tensor(
                    BIG[:, c:c + NT], silu_sb[:, sb * 512:(sb + 1) * 512],
                    Pb[2 + n][:, :], ALU.mult), [ts_si, tu])
                bank_free[2 + n] = [tv]
                silu_free[sb] = [tv]
                big_toks.append(tv)
        big_ready = big_toks[-1]
        wb_free4 = [[big_ready], [big_ready]]
        a1in_free = []
        a1out_free = [[], []]
        for h in range(HC):
            wb = h % 2
            P.dma("gpsimd", WB3[:, wb * 8192: wb * 8192 + FC_P * 128],
                  w_fd_p[h, part], wb_free4[wb], s_w[wb])
            wtok = (s_w[wb], s_w[wb].n)
            if part > 0:
                P.dma("sync", a1in[:, :],
                      part_in[part][h], a1in_free, s_i[wb])
                a1tok = (s_i[wb], s_i[wb].n)
            ev_toks = []
            for n in range(NHALF):
                pbi = 2 * wb + n
                pairs = [(WB3[:, wb * 8192 + fc * 128: wb * 8192 + (fc + 1) * 128],
                          BIG[:, fc * T_CORE + n * NT: fc * T_CORE + (n + 1) * NT])
                         for fc in range(FC_P)]
                tm = mm_group(Pb[pbi][:, :], pairs,
                              [wtok, big_ready], pbi)
                if part == 0:
                    tv = V(lambda e, pbi=pbi, wb=wb, n=n: e.tensor_copy(
                        a1out[:, wb * T_CORE + n * NT: wb * T_CORE + (n + 1) * NT],
                        Pb[pbi][:, :]), [tm] + a1out_free[wb])
                else:
                    tv = V(lambda e, pbi=pbi, wb=wb, n=n: e.tensor_tensor(
                        a1out[:, wb * T_CORE + n * NT: wb * T_CORE + (n + 1) * NT],
                        Pb[pbi][:, :],
                        a1in[:, n * NT:(n + 1) * NT],
                        ALU.add), [tm, a1tok] + a1out_free[wb])
                bank_free[pbi] = [tv]
                ev_toks.append(tv)
            wb_free4[wb] = [ev_toks[-1]]
            if part > 0:
                a1in_free = [ev_toks[-1]]
            P.dma("sync", part_out[part][h],
                  a1out[:, wb * T_CORE:(wb + 1) * T_CORE], ev_toks, s_o[wb])
            a1out_free[wb] = [(s_o[wb], s_o[wb].n)]
        wbf_carry = wb_free4[0] + wb_free4[1]
        P.barrier(s_bar, DMA_SEMS, {"tensor": s_t, "vector": s_v, "scalar": s_s})

    ph_close()
    ph_open()
    WB5 = W_glob
    a1in5 = sbr("a1in5", [128, T_CORE], F32)
    a_ring = sbr("a_ring", [128, 2 * T_CORE], F32)
    sq5 = sbr("sq5", [128, 2 * 512], F32)
    rtm5 = W_glob[:, 8192:10240].bitcast(F32)
    sc_tmp_flat = sc_tmp.rearrange("m s p t -> (m s) p t")
    pass_in5 = [sc_a1, sc_a1A]
    pass_out5 = [sc_a1A, sc_a]
    last_stats = None
    for qp in range(2):
        qbase = qp * 28
        prev_mm_tok = [] if qp == 0 else [big5_last_mm]
        for blk in range(4):
            s0_ = qbase + blk * 7
            src = sc_tmp_flat[s0_:s0_ + 7].rearrange("c p t -> p c t")
            dst = BIG[:, blk * 7 * T_CORE:(blk + 1) * 7 * T_CORE].rearrange(
                "p (c t) -> p c t", c=7)
            P.dma("sync", dst, src,
                  (tmp_spill_done if (qp == 0 and blk == 0) else []) +
                  prev_mm_tok, s_j[blk % 2])
        tmp_rd = (s_j[0], s_j[0].n)
        tmp_rd2 = (s_j[1], s_j[1].n)
        wb_free5 = [list(wbf_carry), list(wbf_carry)]
        a1in_free5 = []
        rtm_free = []
        a_free = [[], []]
        sq_free5 = [[], []]
        pending5 = [None, None]
        stats_guard5 = [[], []]
        for h in range(HC):
            wb = h % 2
            src_w = w_cat_p[h][:, qbase * 128:(qbase + 28) * 128]
            P.dma("gpsimd", WB5[:, wb * 3584:(wb + 1) * 3584], src_w,
                  wb_free5[wb], s_w[wb])
            wtok = (s_w[wb], s_w[wb].n)
            P.dma("sync", a1in5[:, :],
                  pass_in5[qp][h], a1in_free5, s_i[wb])
            a1tok = (s_i[wb], s_i[wb].n)
            ev_toks = []
            for n in range(NHALF):
                pbi = 2 * wb + n
                pairs = [(WB5[:, wb * 3584 + q * 128: wb * 3584 + (q + 1) * 128],
                          BIG[:, q * T_CORE + n * NT: q * T_CORE + (n + 1) * NT])
                         for q in range(28)]
                tm = mm_group(Pb[pbi][:, :], pairs, [wtok, tmp_rd, tmp_rd2], pbi)
                big5_last_mm = tm
                if qp == 1 and pending5[n] is not None:
                    entries, swaits, pwb = pending5[n]
                    last_stats = stats_mms(entries, swaits)
                    sq_free5[n] = [last_stats]
                    stats_guard5[pwb] = [last_stats]
                    pending5[n] = None
                tv = V(lambda e, pbi=pbi, wb=wb, n=n: e.tensor_tensor(
                    a_ring[:, wb * T_CORE + n * NT: wb * T_CORE + (n + 1) * NT],
                    Pb[pbi][:, :],
                    a1in5[:, n * NT:(n + 1) * NT],
                    ALU.add), [tm, a1tok] + a_free[wb] + stats_guard5[wb])
                bank_free[pbi] = [tv]
                if qp == 1:
                    tsq = S(lambda e, wb=wb, n=n: e.activation(
                        sq5[:, n * 512:(n + 1) * 512],
                        a_ring[:, wb * T_CORE + n * NT: wb * T_CORE + (n + 1) * NT],
                        AF.Square), [tv] + sq_free5[n])
                    pending5[n] = (
                        [(st0, n * NT, (n + 1) * NT,
                          a_ring[:, wb * T_CORE + n * NT: wb * T_CORE + (n + 1) * NT]),
                         (st1, n * NT, (n + 1) * NT,
                          sq5[:, n * 512:(n + 1) * 512])], [tv, tsq], wb)
                ev_toks.append(tv)
            wb_free5[wb] = [ev_toks[-1]]
            a1in_free5 = [ev_toks[-1]]
            if qp == 0:
                P.dma("sync", pass_out5[qp][h],
                      a_ring[:, wb * T_CORE:(wb + 1) * T_CORE], ev_toks,
                      s_o[wb])
                a_free[wb] = [(s_o[wb], s_o[wb].n)]
            else:
                tm_evs = []
                for rnd in range(2):
                    pbt = 2 * wb + rnd
                    def tpa(eng, wb=wb, rnd=rnd, pbt=pbt):
                        inst = None
                        for piece in range(4):
                            tt_ = 4 * rnd + piece
                            inst = eng.transpose(
                                Pb[pbt][:, piece * 128:(piece + 1) * 128],
                                a_ring[:, wb * T_CORE + tt_ * 128:
                                       wb * T_CORE + (tt_ + 1) * 128],
                                ident[:, :])
                        return inst
                    P.op("tensor", tpa, ev_toks + bank_free[pbt] + rtm_free,
                         inc=s_t)
                    t_tp = (s_t, s_t.n)
                    tm_ev = V(lambda e, rnd=rnd, pbt=pbt: e.tensor_copy(
                        rtm5[:, rnd * 512:(rnd + 1) * 512], Pb[pbt][:, :]),
                        [t_tp])
                    bank_free[pbt] = [tm_ev]
                    tm_evs.append(tm_ev)
                dst_tm = sc_a[:, :, h * 128:(h + 1) * 128].rearrange(
                    "c p t -> p c t")
                P.dma("sync", dst_tm,
                      rtm5[:, :].rearrange("p (c t) -> p c t", c=8),
                      tm_evs, s_o[wb])
                rtm_free = [(s_o[wb], s_o[wb].n)]
                a_free[wb] = tm_evs + [(s_o[wb], s_o[wb].n)]
        if qp == 1:
            for n in range(NHALF):
                if pending5[n] is not None:
                    entries, swaits, pwb = pending5[n]
                    last_stats = stats_mms(entries, swaits)
                    pending5[n] = None
        wbf_carry = wb_free5[0] + wb_free5[1]
        P.barrier(s_bar, DMA_SEMS, {"tensor": s_t, "vector": s_v, "scalar": s_s})
    lna_tok = ln_cols(0, last_stats, D)
    st0.reset()
    st1.reset()
    P.barrier(s_bar, DMA_SEMS, {"tensor": s_t, "vector": s_v, "scalar": s_s})

    ph_close()
    stack.close()

    ph_open()
    xraw6 = sbr("xraw6", [128, D], F32)
    a_fm = sbr("a_fm", [128, D], F32)
    ev6r = sbr("ev6r", [128, 2 * 512], F32)
    x2t = sbr("x2t", [128, 2 * D], F32)
    xlnb6 = sbr("xlnb6", [128, 2048], BF16)
    stc6 = sbr("stc6", [128, 20], F32)
    x_free6 = []
    a_free6 = []
    atm_free = [[], []]
    x2t_free = [[], []]
    xlnb_free = []
    stc6_free = [[], []]
    for tt in range(8):
        b = tt % 2
        o = 9 * b
        P.dma("sync", a_fm[:, :], sc_a[tt],
              a_free6, s_j[b])
        lda = (s_j[b], s_j[b].n)
        P.dma("sync", xraw6[:, :], x_in[tt * 128:(tt + 1) * 128, :],
              x_free6, s_i[0])
        ldx = (s_i[0], s_i[0].n)
        ev_toks6 = []
        for g in range(8):
            ev = S(lambda e, b=b, g=g, tt=tt: e.activation(
                ev6r[:, (g % 2) * 512:(g % 2 + 1) * 512],
                a_fm[:, g * 512:(g + 1) * 512],
                AF.Identity, bias=lncol[:, 8 + tt:9 + tt],
                scale=lncol[:, tt:tt + 1]),
                [lda, lna_tok] + atm_free[g % 2])
            tadd = V(lambda e, b=b, g=g: e.tensor_tensor(
                x2t[:, b * D + g * 512: b * D + (g + 1) * 512],
                ev6r[:, (g % 2) * 512:(g % 2 + 1) * 512],
                xraw6[:, g * 512:(g + 1) * 512], ALU.add),
                [ev, ldx] + x2t_free[b])
            atm_free[g % 2] = [tadd]
            ev_toks6.append(tadd)
        t_x2 = ev_toks6[-1]
        tsum = V(lambda e, b=b, o=o: e.reduce_sum(
            stc6[:, o:o + 1], x2t[:, b * D:(b + 1) * D],
            mybir.AxisListType.X), stc6_free[b])
        tsq = S(lambda e, b=b, o=o: e.activation(
            xraw6[:, :], x2t[:, b * D:(b + 1) * D], AF.Square,
            accum_out=stc6[:, o + 1:o + 2]), [t_x2])
        a_free6 = [ev_toks6[-1]]
        x_free6 = [t_x2, tsq]
        V(lambda e, o=o: e.tensor_scalar(stc6[:, o + 2:o + 3], stc6[:, o:o + 1],
                                         1.0 / D, None, ALU.mult), [tsum, tsq])
        V(lambda e, o=o: e.tensor_scalar(stc6[:, o + 3:o + 4],
                                         stc6[:, o + 1:o + 2], 1.0 / D, None,
                                         ALU.mult))
        V(lambda e, o=o: e.tensor_tensor(stc6[:, o + 4:o + 5],
                                         stc6[:, o + 2:o + 3],
                                         stc6[:, o + 2:o + 3], ALU.mult))
        V(lambda e, o=o: e.scalar_tensor_tensor(
            stc6[:, o + 5:o + 6], stc6[:, o + 3:o + 4], EPS,
            stc6[:, o + 4:o + 5], ALU.add, ALU.subtract))
        tv6 = V(lambda e, o=o: e.reciprocal(stc6[:, o + 6:o + 7],
                                            stc6[:, o + 5:o + 6]))
        ts6 = S(lambda e, o=o, tt=tt: e.activation(
            lncol[:, 16 + tt:17 + tt], stc6[:, o + 6:o + 7], AF.Sqrt), [tv6])
        tmb6 = V(lambda e, o=o, tt=tt: e.scalar_tensor_tensor(
            lncol[:, 24 + tt:25 + tt], stc6[:, o + 2:o + 3], -1.0,
            lncol[:, 16 + tt:17 + tt], ALU.mult, ALU.mult), [ts6])
        stc6_free[b] = [tmb6]
        spill_toks = []
        last_tpx = None
        for hh in range(2):
            t_xlb = V(lambda e, b=b, hh=hh, tt=tt: e.tensor_scalar(
                xlnb6[:, :], x2t[:, b * D + hh * 2048: b * D + (hh + 1) * 2048],
                lncol[:, 16 + tt:17 + tt],
                lncol[:, 24 + tt:25 + tt], ALU.mult, ALU.add),
                [tmb6] + xlnb_free)
            for g in range(4 * hh, 4 * hh + 4):
                pbi = 2 + g % 2
                def tpx(eng, g=g, hh=hh, pbi=pbi):
                    inst = None
                    pbb = Pb[pbi][:, :].bitcast(BF16)
                    for jj in range(4):
                        c_loc = (4 * g + jj) - hh * 16
                        inst = eng.transpose(pbb[:, jj * 128:(jj + 1) * 128],
                                             xlnb6[:, c_loc * 128:
                                                   (c_loc + 1) * 128],
                                             identb[:, :])
                    return inst
                P.op("tensor", tpx, [t_xlb] + bank_free[pbi], inc=s_t)
                tpt = (s_t, s_t.n)
                last_tpx = tpt
                n_ = tt // 4
                j_ = tt % 4
                dst = XL[:, :].rearrange("p (c t) -> p c t", c=HC)[
                    :, 4 * g:4 * g + 4,
                    n_ * NT + j_ * 128: n_ * NT + (j_ + 1) * 128]
                evx = S(lambda e, dst=dst, pbi=pbi: e.activation(
                    dst, Pb[pbi][:, :].bitcast(BF16)[:, 0:512].rearrange(
                        "p (c t) -> p c t", c=4), AF.Copy), [tpt])
                bank_free[pbi] = [evx]
            xlnb_free = [last_tpx]
        P.dma("sync", sc_x2[tt], x2t[:, b * D:(b + 1) * D],
              [t_x2, tsq, tsum, last_tpx], s_o[b])
        x2t_free[b] = [(s_o[b], s_o[b].n)]
    P.barrier(s_bar, DMA_SEMS, {"tensor": s_t, "vector": s_v, "scalar": s_s})
    ph_close()

    ph_open()
    WB8 = W_glob
    lg_fm = sbr("lg_fm", [8, T_CORE], F32)
    h_sb = sbr("h_sb", [128, T_CORE], F32)
    lg_tm = sbr("lg_tm", [128, 64], F32)
    m1_sb = sbr("m1_sb", [128, 8], F32)
    m2_sb = sbr("m2_sb", [128, 8], F32)
    d1_sb = sbr("d1_sb", [128, 64], F32)
    d2_sb = sbr("d2_sb", [128, 64], F32)
    msk_sb = sbr("msk_sb", [128, 64], F32)
    e2_sb = sbr("e2_sb", [128, 8], F32)
    den_sb = sbr("den_sb", [128, 8], F32)
    w1_sb = sbr("w1_sb", [128, 8], F32)
    w2_sb = sbr("w2_sb", [128, 8], F32)
    wtm_sb = sbr("wtm_sb", [128, 64], F32)
    wfm_sb = sbr("wfm_sb", [8, T_CORE], F32)
    hw_sb = sbr("hw_sb", [128, T_CORE], BF16)
    f_ring = sbr("f_ring", [128, 2 * T_CORE], F32)
    sq8 = sbr("sq8", [128, 2 * 512], F32)
    P.dma("gpsimd", asb8[:, :].rearrange("p (c f) -> p c f", c=HC),
          a_T.rearrange("(c p) f -> p c f", p=128), [], s_k2)
    P.dma("gpsimd", wr8[:, :].rearrange("p (c f) -> p c f", c=HC),
          w_rT.rearrange("(c p) f -> p c f", p=128), [], s_k2)
    P.dma("gpsimd", bfl8[:, :], b_fl, [], s_k2)
    pre = (s_k2, s_k2.n)
    for n in range(NHALF):
        pairs = [(wr8[:, hc * 8:(hc + 1) * 8], xl(hc, n)) for hc in range(HC)]
        tl = mm_group(Pb[n][0:8, :], pairs, [pre], n)
        tvl = V(lambda e, n=n: e.tensor_copy(lg_fm[:, n * NT:(n + 1) * NT],
                                             Pb[n][0:8, :]), [tl])
        bank_free[n] = [tvl]
        pairs_h = [(asb8[:, hc * 128:(hc + 1) * 128], xl(hc, n))
                   for hc in range(HC)]
        th = mm_group(Pb[2 + n][:, :], pairs_h, [pre], 2 + n)
        tvh = V(lambda e, n=n: e.tensor_copy(h_sb[:, n * NT:(n + 1) * NT],
                                             Pb[2 + n][:, :]), [th])
        bank_free[2 + n] = [tvh]
    def tplg(eng):
        inst = None
        for tt in range(8):
            inst = eng.transpose(Pb[0][:, tt * 8:(tt + 1) * 8],
                                 lg_fm[0:8, tt * 128:(tt + 1) * 128],
                                 ident[0:8, 0:8])
        return inst
    P.op("tensor", tplg, [tvl] + bank_free[0], inc=s_t)
    t_lgt = (s_t, s_t.n)
    tv = V(lambda e: e.tensor_copy(lg_tm[:, :], Pb[0][:, 0:64]), [t_lgt])
    bank_free[0] = [tv]
    lg3 = lambda ap: ap.rearrange("p (t e) -> p t e", e=8)
    V(lambda e: e.tensor_reduce(m1_sb[:, :], lg3(lg_tm[:, :]),
                                mybir.AxisListType.X, ALU.max))
    m1b = m1_sb[:, :].unsqueeze(2).broadcast_to([128, 8, 8])
    V(lambda e: e.tensor_tensor(lg3(d1_sb[:, :]), lg3(lg_tm[:, :]), m1b,
                                ALU.is_equal))
    V(lambda e: e.scalar_tensor_tensor(msk_sb[:, :], d1_sb[:, :], -1e30,
                                       lg_tm[:, :], ALU.mult, ALU.add))
    V(lambda e: e.tensor_reduce(m2_sb[:, :], lg3(msk_sb[:, :]),
                                mybir.AxisListType.X, ALU.max))
    tvm = V(lambda e: e.tensor_tensor(den_sb[:, :], m2_sb[:, :],
                                      m1_sb[:, :], ALU.subtract))
    tse = S(lambda e: e.activation(e2_sb[:, :], den_sb[:, :], AF.Exp),
            [tvm])
    V(lambda e: e.tensor_scalar(den_sb[:, :], e2_sb[:, :], 1.0, None,
                                ALU.add), [tse])
    V(lambda e: e.reciprocal(w1_sb[:, :], den_sb[:, :]))
    tvw = V(lambda e: e.tensor_tensor(w2_sb[:, :], w1_sb[:, :], e2_sb[:, :],
                                      ALU.mult))
    m2b = m2_sb[:, :].unsqueeze(2).broadcast_to([128, 8, 8])
    V(lambda e: e.tensor_tensor(lg3(d2_sb[:, :]), lg3(msk_sb[:, :]), m2b,
                                ALU.is_equal))
    w1b = w1_sb[:, :].unsqueeze(2).broadcast_to([128, 8, 8])
    w2b = w2_sb[:, :].unsqueeze(2).broadcast_to([128, 8, 8])
    V(lambda e: e.tensor_tensor(lg3(wtm_sb[:, :]), lg3(d1_sb[:, :]), w1b,
                                ALU.mult), [tvw])
    V(lambda e: e.tensor_tensor(lg3(d2_sb[:, :]), lg3(d2_sb[:, :]), w2b,
                                ALU.mult))
    tv_wtm = V(lambda e: e.tensor_tensor(wtm_sb[:, :], wtm_sb[:, :],
                                         d2_sb[:, :], ALU.add))
    def tpw(eng):
        inst = None
        for tt in range(8):
            dst = Pb[tt // 4][0:8, (tt % 4) * 128:(tt % 4 + 1) * 128]
            inst = eng.transpose(dst, wtm_sb[:, tt * 8:(tt + 1) * 8],
                                 ident[:, :])
        return inst
    P.op("tensor", tpw, [tv_wtm] + bank_free[0] + bank_free[1], inc=s_t)
    t_wfm = (s_t, s_t.n)
    tv_wfm0 = V(lambda e: e.tensor_copy(wfm_sb[0:8, 0:512], Pb[0][0:8, :]),
                [t_wfm])
    tv_wfm = V(lambda e: e.tensor_copy(wfm_sb[0:8, 512:1024], Pb[1][0:8, :]))
    bank_free[0] = [tv_wfm0]
    bank_free[1] = [tv_wfm]
    for n in range(NHALF):
        tsel = mm_group(Pb[2 + n][:, :],
                        [(esel_sb[:, :], wfm_sb[0:8, n * NT:(n + 1) * NT])],
                        [tv_wfm], 2 + n)
        thw = V(lambda e, n=n: e.tensor_tensor(
            hw_sb[:, n * NT:(n + 1) * NT], h_sb[:, n * NT:(n + 1) * NT],
            Pb[2 + n][:, :], ALU.mult), [tsel, tvh])
        bank_free[2 + n] = [thw]
    hw_tok = (s_v, s_v.n)
    rtm8 = W_glob[:, 8192:10240].bitcast(F32)
    rtm_free8 = []
    wb_free8 = [list(wbf_carry), list(wbf_carry)]
    f_free = [[], []]
    sq_free8 = [[], []]
    pending8 = [None, None]
    stats_guard8 = [[], []]
    last_stats8 = None
    for h in range(HC):
        wb = h % 2
        P.dma("gpsimd", WB8[:, wb * 4096:(wb + 1) * 4096], w_b_p[h],
              wb_free8[wb], s_w[wb])
        wtok = (s_w[wb], s_w[wb].n)
        ev_toks = []
        for n in range(NHALF):
            pbi = 2 * wb + n
            pairs = [(WB8[:, wb * 4096 + hc * 128: wb * 4096 + (hc + 1) * 128],
                      xl(hc, n)) for hc in range(HC)]
            pairs.append((bfl8[:, h * 128:(h + 1) * 128],
                          hw_sb[:, n * NT:(n + 1) * NT]))
            tm = mm_group(Pb[pbi][:, :], pairs, [wtok, hw_tok, pre], pbi)
            if pending8[n] is not None:
                entries, swaits, pwb = pending8[n]
                last_stats8 = stats_mms(entries, swaits)
                sq_free8[n] = [last_stats8]
                stats_guard8[pwb] = [last_stats8]
                pending8[n] = None
            tv = V(lambda e, pbi=pbi, wb=wb, n=n, h=h: e.tensor_scalar(
                f_ring[:, wb * T_CORE + n * NT: wb * T_CORE + (n + 1) * NT],
                Pb[pbi][:, :], bb_sb[:, h:h + 1], None, ALU.add),
                [tm] + f_free[wb] + stats_guard8[wb])
            bank_free[pbi] = [tv]
            tsq = S(lambda e, wb=wb, n=n: e.activation(
                sq8[:, n * 512:(n + 1) * 512],
                f_ring[:, wb * T_CORE + n * NT: wb * T_CORE + (n + 1) * NT],
                AF.Square), [tv] + sq_free8[n])
            pending8[n] = (
                [(st0, n * NT, (n + 1) * NT,
                  f_ring[:, wb * T_CORE + n * NT: wb * T_CORE + (n + 1) * NT]),
                 (st1, n * NT, (n + 1) * NT,
                  sq8[:, n * 512:(n + 1) * 512])], [tv, tsq], wb)
            ev_toks.append(tv)
        wb_free8[wb] = [ev_toks[-1]]
        tm_evs8 = []
        for rnd in range(2):
            pbt = 2 * wb + rnd
            def tpf(eng, wb=wb, rnd=rnd, pbt=pbt):
                inst = None
                for piece in range(4):
                    tt_ = 4 * rnd + piece
                    inst = eng.transpose(
                        Pb[pbt][:, piece * 128:(piece + 1) * 128],
                        f_ring[:, wb * T_CORE + tt_ * 128:
                               wb * T_CORE + (tt_ + 1) * 128],
                        ident[:, :])
                return inst
            P.op("tensor", tpf, ev_toks + bank_free[pbt] + rtm_free8,
                 inc=s_t)
            t_tp = (s_t, s_t.n)
            tm_ev = V(lambda e, rnd=rnd, pbt=pbt: e.tensor_copy(
                rtm8[:, rnd * 512:(rnd + 1) * 512], Pb[pbt][:, :]), [t_tp])
            bank_free[pbt] = [tm_ev]
            tm_evs8.append(tm_ev)
        dst_tm = sc_f[:, :, h * 128:(h + 1) * 128].rearrange("c p t -> p c t")
        P.dma("sync", dst_tm, rtm8[:, :].rearrange("p (c t) -> p c t", c=8),
              tm_evs8, s_o[wb])
        rtm_free8 = [(s_o[wb], s_o[wb].n)]
        f_free[wb] = tm_evs8 + [(s_o[wb], s_o[wb].n)]
    for n in range(NHALF):
        if pending8[n] is not None:
            entries, swaits, pwb = pending8[n]
            last_stats8 = stats_mms(entries, swaits)
            pending8[n] = None
    ln3_tok = ln_cols(32, last_stats8, D)
    st0.reset()
    st1.reset()
    P.barrier(s_bar, DMA_SEMS, {"tensor": s_t, "vector": s_v, "scalar": s_s})

    ph_close()
    p_stack.close()
    w_stack.close()
    ph_open()
    x2t9 = sbr("x2t9", [128, 2 * D], F32)
    ft9 = sbr("ft9", [128, 2 * D], F32)
    lnf = sbr("lnf", [128, D], F32)
    oring = sbr("oring", [128, 2 * D], F32)
    x2_free9 = [[], []]
    f_free9 = [[], []]
    lnf_free = []
    o_free9 = [[], []]
    for tt in range(8):
        b = tt % 2
        P.dma("sync", x2t9[:, b * D:(b + 1) * D], sc_x2[tt],
              x2_free9[b], s_i[b])
        ldx2 = (s_i[b], s_i[b].n)
        P.dma("sync", ft9[:, b * D:(b + 1) * D], sc_f[tt],
              f_free9[b], s_j[b])
        ldf = (s_j[b], s_j[b].n)
        t_ln = S(lambda e, b=b, tt=tt: e.activation(
            lnf[:, :], ft9[:, b * D:(b + 1) * D],
            AF.Identity, bias=lncol[:, 40 + tt:41 + tt],
            scale=lncol[:, 32 + tt:33 + tt]),
            [ldf, ln3_tok] + lnf_free)
        t_out = V(lambda e, b=b: e.tensor_tensor(
            oring[:, b * D:(b + 1) * D], lnf[:, :],
            x2t9[:, b * D:(b + 1) * D], ALU.add),
            [ldx2, t_ln] + o_free9[b])
        lnf_free = [t_out]
        x2_free9[b] = [t_out]
        f_free9[b] = [t_ln]
        P.dma("sync", out_hbm[tt * 128:(tt + 1) * 128, :],
              oring[:, b * D:(b + 1) * D], [t_out], s_o[b])
        o_free9[b] = [(s_o[b], s_o[b].n)]
    ph_close()
    P.op("sync", lambda e: e.nop(), [(s, s.n) for s in ALL_DMA_SEMS])
    P.op("gpsimd", lambda e: e.nop(), [(s_o[0], s_o[0].n), (s_o[1], s_o[1].n)])

    P.emit(cut=_CUT[0] if _CUT else None)
    return nc



def _prep_host(inputs):
    c = np.ascontiguousarray
    up_w = inputs["L0_attn_up_w"]
    gate_w = inputs["L0_attn_gate_w"]
    lp = inputs["L0_attn_local_point"]
    lao = inputs["L0_attn_local_approx_output"]
    fuse = inputs["L0_attn_fuse_weight"]
    df = inputs["L0_attn_discount_factor"]
    fg = inputs["L0_attn_ffn_gate_w"]
    fu = inputs["L0_attn_ffn_up_w"]
    fd = inputs["L0_attn_ffn_down_w"]
    wb = inputs["L0_ffn_W_base"]
    bb = inputs["L0_ffn_b_base"]
    wr = inputs["L0_ffn_W_router"]
    A = inputs["L0_ffn_A"]
    B = inputs["L0_ffn_B"]
    f32 = np.float32
    def tile_kM(wT, steps):
        kt, mt = wT.shape
        return c(wT.reshape(kt // 128, 128, steps, mt // steps)
                 .transpose(2, 1, 0, 3).reshape(steps, 128, -1))

    w_catT = np.concatenate(
        [np.asarray(lao, f32).T] +
        [np.asarray(fuse[:, :, m], f32).T for m in range(6)], axis=0)
    fdT = np.asarray(fd, f32).T
    shared = {
        "w_up_p": tile_kM(np.asarray(up_w, f32).T, SC),
        "w_gate_p": tile_kM(np.asarray(gate_w, f32).T, SC),
        "lp_pc": c(np.asarray(lp, f32).reshape(SC, 128).T),
        "df_bc": c(np.broadcast_to(np.asarray(df, f32), (128, 6))),
        "w_cat_p": tile_kM(w_catT, HC),
        "w_fg_p": tile_kM(np.asarray(fg, f32).T, 78),
        "w_fu_p": tile_kM(np.asarray(fu, f32).T, 78),
        "w_fd_p": c(fdT.reshape(3, FC_P, 128, HC, 128)
                    .transpose(3, 0, 2, 1, 4).reshape(HC, 3, 128, FC_P * 128)),
        "w_b_p": tile_kM(np.asarray(wb, f32).T, HC),
        "bb_pc": c(np.asarray(bb, f32).reshape(HC, 128).T),
        "w_rT": c(np.asarray(wr, f32).T),
        "a_T": c(np.asarray(A, f32).reshape(128, D).T),
        "b_fl": c(np.asarray(B, f32).transpose(0, 2, 1).reshape(128, D)),
    }
    return shared


def kernel(**inputs):
    x = np.asarray(inputs["L0_attn_hidden_states"], np.float32).reshape(8192, D)
    shared = _prep_host(inputs)
    nc = build_core_program(_DEBUG_DUMP)
    in_maps = []
    for cid in range(8):
        m = dict(shared)
        m["x_in"] = np.ascontiguousarray(x[cid * T_CORE:(cid + 1) * T_CORE])
        in_maps.append(m)
    res = run_bass_kernel_spmd(nc, in_maps, core_ids=list(range(8)))
    out = np.concatenate([res.results[cid]["out"] for cid in range(8)], axis=0)
    if _DEBUG_DUMP:
        kernel._dbg = res.results
    return out.reshape(1, 8192, D)


# version 11
# speedup vs baseline: 1.0301x; 1.0301x over previous
import math
import numpy as np
import concourse.bass as bass
import concourse.mybir as mybir
from concourse.bass_utils import run_bass_kernel_spmd

F32 = mybir.dt.float32
BF16 = mybir.dt.bfloat16
AF = mybir.ActivationFunctionType
ALU = mybir.AluOpType

ENGINES = ("tensor", "vector", "scalar", "sync", "gpsimd")

T_CORE = 1024
NHALF = 2
NT = 512
HC = 32
SC = 8
FC_P = 26
D = 4096
S_DIM = 1024
F_DIM = 9984
EPS = 1e-5
THD = 2.5

_DEBUG_DUMP = ()
_CUT = []


class Sem:
    __slots__ = ("h", "n", "name")

    def __init__(self, h, name):
        self.h = h
        self.n = 0
        self.name = name


class Prog:
    def __init__(self, nc):
        self.nc = nc
        self.q = {e: [] for e in ENGINES}
        self.sems = []
        self.cuts = []

    def sem(self, name):
        s = Sem(self.nc.alloc_semaphore(name), name)
        self.sems.append(s)
        return s

    def op(self, eng, fn, waits=(), inc=None, inc16=None):
        incs = []
        if inc is not None:
            inc.n += 1
            incs.append((inc, 1))
        if inc16 is not None:
            inc16.n += 16
            incs.append((inc16, 16))
        self.q[eng].append((tuple(waits), fn, tuple(incs)))

    def dma(self, eng, out_ap, in_ap, waits, sem):
        def fn(e, o=out_ap, i=in_ap):
            return e.dma_start(o, i)
        self.op(eng, fn, waits, inc16=sem)

    def barrier(self, sbar, dma_sems, eng_sems=()):
        eng_sems = dict(eng_sems)
        dma_waits = [(s, s.n) for s in dma_sems]
        for e in ENGINES:
            if e == "gpsimd":
                continue
            w = list(dma_waits) if e == "sync" else []
            if e in eng_sems:
                s = eng_sems[e]
                w.append((s, s.n))
            self.op(e, lambda eng: eng.nop(), w, inc=sbar)
        tgt = sbar.n
        for e in ENGINES:
            if e == "gpsimd":
                continue
            self.op(e, lambda eng: eng.nop(), [(sbar, tgt)])
        self.cuts.append({e: len(self.q[e]) for e in ENGINES})

    def emit(self, cut=None):
        if cut is not None:
            lim = self.cuts[cut]
            self.q = {e: self.q[e][:lim[e]] for e in ENGINES}
        def runner(items):
            def run(eng):
                for waits, fn, incs in items:
                    for (sem, val) in waits:
                        eng.wait_ge(sem.h, val)
                    inst = fn(eng)
                    for (sem, k) in incs:
                        inst = inst.then_inc(sem.h, k)
            return run

        with self.nc.Block() as block:
            block.tensor(runner(self.q["tensor"]))
            block.vector(runner(self.q["vector"]))
            block.scalar(runner(self.q["scalar"]))
            block.sync(runner(self.q["sync"]))
            block.gpsimd(runner(self.q["gpsimd"]))


class StatsBank:

    def __init__(self, ap, n_banks=2):
        self.ap = ap
        self.touched = [False] * n_banks

    def reset(self):
        for i in range(len(self.touched)):
            self.touched[i] = False

    def start_flag(self, col0, col1):
        b = col0 // 512
        assert col1 <= (b + 1) * 512
        st = not self.touched[b]
        self.touched[b] = True
        return st


def build_core_program(debug_dump=()):
    nc = bass.Bass("TRN2", target_bir_lowering=False)

    def din(name, shape):
        return nc.dram_tensor(name, shape, F32, kind="ExternalInput").ap()

    x_in = din("x_in", [T_CORE, D])
    w_up_p = din("w_up_p", [SC, 128, HC * 128])
    w_gate_p = din("w_gate_p", [SC, 128, HC * 128])
    lp_pc = din("lp_pc", [128, SC])
    df_bc = din("df_bc", [128, 6])
    w_cat_p = din("w_cat_p", [HC, 128, 56 * 128])
    w_fg_p = din("w_fg_p", [78, 128, HC * 128])
    w_fu_p = din("w_fu_p", [78, 128, HC * 128])
    w_fd_p = din("w_fd_p", [HC, 3, 128, FC_P * 128])
    w_b_p = din("w_b_p", [HC, 128, HC * 128])
    bb_pc = din("bb_pc", [128, HC])
    w_rT = din("w_rT", [D, 8])
    a_T = din("a_T", [D, 128])
    b_fl = din("b_fl", [128, D])

    out_hbm = nc.dram_tensor("out", [T_CORE, D], F32, kind="ExternalOutput").ap()

    def scratch(name, shape, dtype=F32):
        kind = "ExternalOutput" if name in debug_dump else "Internal"
        return nc.dram_tensor(name, shape, dtype, kind=kind).ap()

    sc_tmp = scratch("sc_tmp", [7, SC, 128, T_CORE], BF16)
    sc_a1A = scratch("sc_a1A", [HC, 128, T_CORE])
    sc_a1B = scratch("sc_a1B", [HC, 128, T_CORE])
    sc_a1 = scratch("sc_a1", [HC, 128, T_CORE])
    sc_a = scratch("sc_a", [8, 128, D])
    sc_x2 = scratch("sc_x2", [8, 128, D])
    sc_f = scratch("sc_f", [8, 128, D])

    ident_d = nc.inline_tensor(np.eye(128, dtype=np.float32), "ident")
    import ml_dtypes
    identb_d = nc.inline_tensor(np.eye(128).astype(ml_dtypes.bfloat16), "identb")
    ones_row_d = nc.inline_tensor(np.ones((1, 128), np.float32), "ones_row")
    neg_row_d = nc.inline_tensor(np.full((1, 128), -1.0, np.float32), "neg_row")
    ones_col_d = nc.inline_tensor(np.ones((128, 1), np.float32), "ones_col")
    ones_colb_d = nc.inline_tensor(np.ones((128, 1)).astype(ml_dtypes.bfloat16),
                                   "ones_colb")
    esel = np.zeros((8, 128), np.float32)
    for e in range(8):
        esel[e, 16 * e:16 * (e + 1)] = 2.0
    esel_d = nc.inline_tensor(esel, "esel")

    XL = nc.alloc_sbuf_tensor("XL", [128, HC * T_CORE], BF16, side="left")
    ident = nc.alloc_sbuf_tensor("ident_sb", [128, 128], F32, side="left")
    identb = nc.alloc_sbuf_tensor("identb_sb", [128, 128], BF16, side="left")
    lncol = nc.alloc_sbuf_tensor("lncol", [128, 48], F32, side="left")
    ones_row = nc.alloc_sbuf_tensor("ones_row_sb", [1, 128], F32, side="left")
    neg_row = nc.alloc_sbuf_tensor("neg_row_sb", [1, 128], F32, side="left")
    ones_col = nc.alloc_sbuf_tensor("ones_col_sb", [128, 1], F32, side="left")
    ones_colb = nc.alloc_sbuf_tensor("ones_colb_sb", [128, 1], BF16, side="left")
    esel_sb = nc.alloc_sbuf_tensor("esel_sb", [8, 128], F32, side="left")
    lp_sb = nc.alloc_sbuf_tensor("lp_sb", [128, SC], F32, side="left")
    ndf_sb = nc.alloc_sbuf_tensor("ndf_sb", [128, 6], F32, side="left")
    bb_sb = nc.alloc_sbuf_tensor("bb_sb", [128, HC], F32, side="left")
    r_mean = nc.alloc_sbuf_tensor("r_mean", [1, T_CORE], F32, side="left")
    r_tmp = nc.alloc_sbuf_tensor("r_tmp", [1, T_CORE], F32, side="left")
    r_aux = nc.alloc_sbuf_tensor("r_aux", [1, T_CORE], F32, side="left")
    r_mb = nc.alloc_sbuf_tensor("r_mb", [1, T_CORE], F32, side="left")
    bc_t = nc.alloc_sbuf_tensor("bc_t", [128, 512], F32, side="left")

    Pb = [nc.alloc_psum_tensor("P%d" % i, [128, 512], F32) for i in range(4)]
    S0 = nc.alloc_psum_tensor("S0", [1, T_CORE], F32)
    S1 = nc.alloc_psum_tensor("S1", [1, T_CORE], F32)
    st0 = StatsBank(S0)
    st1 = StatsBank(S1)

    P = Prog(nc)
    s_t = P.sem("s_t")
    s_v = P.sem("s_v")
    s_s = P.sem("s_s")
    s_bar = P.sem("s_bar")
    s_w = [P.sem("s_w0"), P.sem("s_w1")]
    s_i = [P.sem("s_i0"), P.sem("s_i1")]
    s_j = [P.sem("s_j0"), P.sem("s_j1")]
    s_o = [P.sem("s_o0"), P.sem("s_o1")]
    s_k = P.sem("s_k")
    s_k2 = P.sem("s_k2")
    DMA_SEMS = tuple(s_i + s_j + s_o + [s_k])
    ALL_DMA_SEMS = tuple(s_w + s_i + s_j + s_o + [s_k, s_k2])

    bank_free = {i: [] for i in range(4)}

    def TOP(fn, waits=(), inc=None):
        P.op("tensor", fn, waits, inc=inc)

    def mm_group(psum_ap, pairs, waits, bank, extra_first_start=True):
        w = list(waits) + bank_free[bank]

        def fn(eng, pairs=pairs, out=psum_ap):
            inst = None
            n = len(pairs)
            for i, (l, r) in enumerate(pairs):
                inst = eng.matmul(out, l, r,
                                  start=(extra_first_start and i == 0),
                                  stop=(i == n - 1), skip_group_check=True)
            return inst
        P.op("tensor", fn, w, inc=s_t)
        return (s_t, s_t.n)

    def stats_mms(entries, waits):
        plan = []
        for (sb, c0, c1, rhs) in entries:
            st = sb.start_flag(c0, c1)
            plan.append((sb.ap[0:1, c0:c1], rhs, st))

        def fn(eng, plan=plan, oc=ones_col, ocb=ones_colb):
            inst = None
            for (outap, rhs, st) in plan:
                ones = ocb if rhs.dtype == BF16 else oc
                inst = eng.matmul(outap, ones[:, :], rhs, start=st, stop=True,
                                  skip_group_check=True)
            return inst
        P.op("tensor", fn, waits, inc=s_t)
        return (s_t, s_t.n)

    def V(fn, waits=(), inc=True):
        w = list(waits) + [(s_v, s_v.n)]
        P.op("vector", fn, w, inc=s_v if inc else None)
        return (s_v, s_v.n)

    def S(fn, waits=(), inc=True):
        w = list(waits) + [(s_s, s_s.n)]
        P.op("scalar", fn, w, inc=s_s if inc else None)
        return (s_s, s_s.n)

    def ln_rows(col0, col1, mm_tok, denom):
        c0, c1 = col0, col1
        V(lambda e: e.tensor_scalar(r_mean[:, c0:c1], S0[0:1, c0:c1],
                                    1.0 / denom, None, ALU.mult), [mm_tok])
        V(lambda e: e.tensor_scalar(r_tmp[:, c0:c1], S1[0:1, c0:c1],
                                    1.0 / denom, None, ALU.mult))
        stats_free = (s_v, s_v.n)
        V(lambda e: e.tensor_tensor(r_aux[:, c0:c1], r_mean[:, c0:c1],
                                    r_mean[:, c0:c1], ALU.mult))
        V(lambda e: e.scalar_tensor_tensor(r_mb[:, c0:c1], r_tmp[:, c0:c1],
                                           EPS, r_aux[:, c0:c1], ALU.add,
                                           ALU.subtract))
        V(lambda e: e.reciprocal(r_aux[:, c0:c1], r_mb[:, c0:c1]))
        tv = (s_v, s_v.n)
        ts = S(lambda e: e.activation(r_tmp[:, c0:c1], r_aux[:, c0:c1],
                                      AF.Sqrt), [tv])
        tmb = V(lambda e: e.tensor_tensor(r_mb[:, c0:c1], r_mean[:, c0:c1],
                                          r_tmp[:, c0:c1], ALU.mult), [ts])
        toks = []
        for cc in range(c0, c1, 512):
            bidx = (cc // 512) % 2
            t1 = mm_group(Pb[2 + bidx][:, :], [(ones_row[:, :], r_tmp[:, cc:cc + 512])],
                          [tmb], 2 + bidx)
            tv1 = V(lambda e, cc=cc, b=bidx: e.tensor_copy(bc_r[:, cc:cc + 512],
                                                           Pb[2 + b][:, :]), [t1])
            bank_free[2 + bidx] = [tv1]
            t2 = mm_group(Pb[2 + bidx][:, :], [(neg_row[:, :], r_mb[:, cc:cc + 512])],
                          [tmb], 2 + bidx)
            tv2 = V(lambda e, cc=cc, b=bidx: e.tensor_copy(bc_mb[:, cc:cc + 512],
                                                           Pb[2 + b][:, :]), [t2])
            bank_free[2 + bidx] = [tv2]
            toks.append(tv2)
        return toks[-1], stats_free

    def ln_cols(dst_base, mm_tok, denom):
        c0, c1 = 0, T_CORE
        V(lambda e: e.tensor_scalar(r_mean[:, c0:c1], S0[0:1, c0:c1],
                                    1.0 / denom, None, ALU.mult), [mm_tok])
        V(lambda e: e.tensor_scalar(r_tmp[:, c0:c1], S1[0:1, c0:c1],
                                    1.0 / denom, None, ALU.mult))
        V(lambda e: e.tensor_tensor(r_aux[:, c0:c1], r_mean[:, c0:c1],
                                    r_mean[:, c0:c1], ALU.mult))
        V(lambda e: e.scalar_tensor_tensor(r_mb[:, c0:c1], r_tmp[:, c0:c1],
                                           EPS, r_aux[:, c0:c1], ALU.add,
                                           ALU.subtract))
        V(lambda e: e.reciprocal(r_aux[:, c0:c1], r_mb[:, c0:c1]))
        tv = (s_v, s_v.n)
        ts = S(lambda e: e.activation(r_tmp[:, c0:c1], r_aux[:, c0:c1],
                                      AF.Sqrt), [tv])
        tmb = V(lambda e: e.scalar_tensor_tensor(
            r_mb[:, c0:c1], r_mean[:, c0:c1], -1.0, r_tmp[:, c0:c1],
            ALU.mult, ALU.mult), [ts])
        def tpc(eng):
            inst = None
            for tt in range(8):
                inst = eng.transpose(Pb[2][:, tt:tt + 1],
                                     r_tmp[0:1, tt * 128:(tt + 1) * 128],
                                     ident[0:1, 0:1])
                inst = eng.transpose(Pb[2][:, 8 + tt:9 + tt],
                                     r_mb[0:1, tt * 128:(tt + 1) * 128],
                                     ident[0:1, 0:1])
            return inst
        P.op("tensor", tpc, [tmb] + bank_free[2], inc=s_t)
        t_c = (s_t, s_t.n)
        evc = V(lambda e, dst_base=dst_base: e.tensor_copy(
            lncol[:, dst_base:dst_base + 16], Pb[2][:, 0:16]), [t_c])
        bank_free[2] = [evc]
        return evc

    def xl(hc, n=None):
        if n is None:
            return XL[:, hc * T_CORE:(hc + 1) * T_CORE]
        return XL[:, hc * T_CORE + n * NT: hc * T_CORE + (n + 1) * NT]

    import contextlib
    stack = contextlib.ExitStack()
    w_stack = contextlib.ExitStack()

    _ph = {}

    def ph_open():
        es = contextlib.ExitStack()
        _ph["es"] = es
        return es

    def sbr(name, shape, dtype=F32):
        return _ph["es"].enter_context(
            nc.sbuf_tensor(name, shape, dtype, side="right"))

    def ph_close():
        _ph["es"].close()

    W_glob = w_stack.enter_context(
        nc.sbuf_tensor("W_glob", [128, 16384], BF16, side="right"))
    p_stack = contextlib.ExitStack()
    asb8 = p_stack.enter_context(
        nc.sbuf_tensor("asb8", [128, HC * 128], BF16, side="right"))
    wr8 = p_stack.enter_context(
        nc.sbuf_tensor("wr8", [128, HC * 8], BF16, side="right"))
    bfl8 = p_stack.enter_context(
        nc.sbuf_tensor("bfl8", [128, D], BF16, side="right"))

    P.dma("sync", ident[:, :], ident_d.ap(), [], s_k)
    P.dma("sync", identb[:, :], identb_d.ap(), [], s_k)
    P.dma("sync", ones_row[:, :], ones_row_d.ap(), [], s_k)
    P.dma("sync", neg_row[:, :], neg_row_d.ap(), [], s_k)
    P.dma("sync", ones_col[:, :], ones_col_d.ap(), [], s_k)
    P.dma("sync", ones_colb[:, :], ones_colb_d.ap(), [], s_k)
    P.dma("sync", esel_sb[:, :], esel_d.ap(), [], s_k)
    P.dma("sync", lp_sb[:, :], lp_pc, [], s_k)
    P.dma("sync", bb_sb[:, :], bb_pc, [], s_k)
    ndf_raw = nc.alloc_sbuf_tensor("ndf_raw", [128, 6], F32, side="left")
    P.dma("sync", ndf_raw[:, :], df_bc, [], s_k)
    tload = (s_k, s_k.n)
    V(lambda e: e.tensor_scalar(ndf_sb[:, :], ndf_raw[:, :], -1.0, None,
                                ALU.mult), [tload])
    P.barrier(s_bar, DMA_SEMS, {"tensor": s_t, "vector": s_v, "scalar": s_s})

    ph_open()
    xraw = sbr("xraw", [128, 2 * D], F32)
    junk1 = sbr("junk1", [128, D], F32)
    stc = sbr("stc", [128, 20], F32)
    tn1 = sbr("tn1", [128, 512], F32)
    xraw_free = [[], []]
    stc_free = [[], []]
    bc_free = [[], []]
    for tt in range(8):
        b = tt % 2
        o = 9 * b
        P.dma("sync", xraw[:, b * D:(b + 1) * D],
              x_in[tt * 128:(tt + 1) * 128, :], xraw_free[b], s_i[b])
        ld = (s_i[b], s_i[b].n)
        tsum = V(lambda e, b=b, o=o: e.reduce_sum(
            stc[:, o + 0:o + 1], xraw[:, b * D:(b + 1) * D], mybir.AxisListType.X),
            [ld])
        tsq = S(lambda e, b=b, o=o: e.activation(
            junk1[:, :], xraw[:, b * D:(b + 1) * D], AF.Square,
            accum_out=stc[:, o + 1:o + 2]), [ld] + stc_free[b])
        V(lambda e, o=o: e.tensor_scalar(stc[:, o + 2:o + 3], stc[:, o + 0:o + 1], 1.0 / D, None,
                                    ALU.mult), [tsum])
        V(lambda e, o=o: e.tensor_scalar(stc[:, o + 3:o + 4], stc[:, o + 1:o + 2], 1.0 / D, None,
                                    ALU.mult), [tsq])
        V(lambda e, o=o: e.tensor_tensor(stc[:, o + 4:o + 5], stc[:, o + 2:o + 3], stc[:, o + 2:o + 3],
                                    ALU.mult))
        V(lambda e, o=o: e.scalar_tensor_tensor(stc[:, o + 5:o + 6], stc[:, o + 3:o + 4], EPS,
                                           stc[:, o + 4:o + 5], ALU.add, ALU.subtract))
        tv = V(lambda e, o=o: e.reciprocal(stc[:, o + 6:o + 7], stc[:, o + 5:o + 6]))
        ts = S(lambda e, o=o: e.activation(stc[:, o + 7:o + 8], stc[:, o + 6:o + 7], AF.Sqrt), [tv])
        tmb = V(lambda e, o=o: e.tensor_tensor(stc[:, o + 8:o + 9], stc[:, o + 2:o + 3],
                                          stc[:, o + 7:o + 8], ALU.mult), [ts])
        def tpr(eng, o=o):
            eng.transpose(Pb[2][0:1, 0:128], stc[:, o + 7:o + 8], ident[:, :])
            return eng.transpose(Pb[2][0:1, 128:256], stc[:, o + 8:o + 9], ident[:, :])
        P.op("tensor", tpr, [tmb] + bank_free[2], inc=s_t)
        t_rows = (s_t, s_t.n)
        ev_rows = V(lambda e: e.tensor_copy(r_tmp[0:1, 0:256], Pb[2][0:1, 0:256]),
                    [t_rows])
        bank_free[2] = [ev_rows]
        def bcast(eng):
            eng.matmul(Pb[3][:, 0:128], ones_row[:, :], r_tmp[0:1, 0:128],
                       start=True, stop=True, skip_group_check=True)
            return eng.matmul(Pb[3][:, 128:256], neg_row[:, :],
                              r_tmp[0:1, 128:256], start=True, stop=True,
                              skip_group_check=True)
        P.op("tensor", bcast, [ev_rows] + bank_free[3] + bc_free[b], inc=s_t)
        t_bc = (s_t, s_t.n)
        ev_bc = V(lambda e, b=b: e.tensor_copy(
            bc_t[:, b * 256:b * 256 + 128], Pb[3][:, 0:128]), [t_bc])
        ev_bc2 = V(lambda e, b=b: e.tensor_copy(
            bc_t[:, b * 256 + 128:b * 256 + 256], Pb[3][:, 128:256]))
        bank_free[3] = [ev_bc2]
        last_tp = None
        for g in range(8):
            pbi = g % 2
            def tp_fn(eng, b=b, g=g, pbi=pbi):
                inst = None
                for jj in range(4):
                    inst = eng.transpose(Pb[pbi][:, jj * 128:(jj + 1) * 128],
                                         xraw[:, b * D + (4 * g + jj) * 128:
                                              b * D + (4 * g + jj + 1) * 128],
                                         ident[:, :])
                return inst
            P.op("tensor", tp_fn, [ld, tsq] + bank_free[pbi], inc=s_t)
            tp = (s_t, s_t.n)
            last_tp = tp
            rrep = bc_t[:, b * 256:b * 256 + 128].unsqueeze(1).broadcast_to(
                [128, 4, 128])
            mrep = bc_t[:, b * 256 + 128:b * 256 + 256].unsqueeze(1).broadcast_to(
                [128, 4, 128])
            t1 = V(lambda e, pbi=pbi, rrep=rrep: e.tensor_tensor(
                tn1[:, :].rearrange("p (c t) -> p c t", c=4),
                Pb[pbi][:, :].rearrange("p (c t) -> p c t", c=4), rrep,
                ALU.mult), [tp, ev_bc2])
            bank_free[pbi] = [t1]
            n = tt // 4
            j = tt % 4
            dst = XL[:, :].rearrange("p (c t) -> p c t", c=HC)[
                :, 4 * g:4 * g + 4, n * NT + j * 128: n * NT + (j + 1) * 128]
            t_xl = V(lambda e, dst=dst, mrep=mrep: e.tensor_tensor(
                dst, tn1[:, :].rearrange("p (c t) -> p c t", c=4), mrep,
                ALU.add))
        xraw_free[b] = [last_tp, tsq, tsum]
        bc_free[b] = [t_xl]
        stc_free[b] = [tmb]
        if tt == 3:
            xl_ready_h0 = [t_xl]
        if tt == 7:
            xl_ready_h1 = [t_xl]
    xl_ready = [xl_ready_h0, xl_ready_h1]

    ph_close()
    if "xl1" in debug_dump:
        dbg_xl = nc.dram_tensor("dbg_xl1", [128, HC * T_CORE], BF16,
                                kind="ExternalOutput").ap()
        P.dma("sync", dbg_xl, XL[:, :], xl_ready[0] + xl_ready[1], s_k)
        P.barrier(s_bar, DMA_SEMS, {"tensor": s_t, "vector": s_v, "scalar": s_s})

    ph_open()
    WB2 = W_glob
    tmpr = sbr("tmpr", [128, 2 * 7 * T_CORE], BF16)
    au_sb = sbr("au_sb", [128, 512], F32)
    aud_sb = sbr("aud_sb", [128, 512], F32)
    delta_sb = sbr("delta_sb", [128, 512], F32)
    absd_sb = sbr("absd_sb", [128, 512], F32)
    logd_sb = sbr("logd_sb", [128, 512], F32)
    keep_sb = sbr("keep_sb", [128, 512], F32)
    dt_sb = sbr("dt_sb", [128, 512], F32)
    dtk_sb = sbr("dtk_sb", [128, 512], F32)
    wb_free = [[], []]
    tmpr_free = [[], []]
    scratch_free = []
    for s in range(SC):
        wb = s % 2
        P.dma("gpsimd", WB2[:, wb * 8192: wb * 8192 + 4096], w_up_p[s],
              wb_free[wb], s_w[wb])
        P.dma("gpsimd", WB2[:, wb * 8192 + 4096: wb * 8192 + 8192],
              w_gate_p[s], [], s_w[wb])
        wtok = (s_w[wb], s_w[wb].n)
        mm_toks = []
        for n in range(NHALF):
            pairs_u = [(WB2[:, wb * 8192 + hc * 128: wb * 8192 + (hc + 1) * 128],
                        xl(hc, n)) for hc in range(HC)]
            t_au = mm_group(Pb[n][:, :], pairs_u, [wtok] + xl_ready[n], n)
            pairs_g = [(WB2[:, wb * 8192 + 4096 + hc * 128:
                            wb * 8192 + 4096 + (hc + 1) * 128],
                        xl(hc, n)) for hc in range(HC)]
            t_gp = mm_group(Pb[2 + n][:, :], pairs_g, [wtok] + xl_ready[n],
                            2 + n)
            mm_toks.append((t_au, t_gp))
        wb_free[wb] = [mm_toks[1][0], mm_toks[1][1]]
        for n in range(NHALF):
            t_au, t_gp = mm_toks[n]
            col = lambda m, wb=wb, n=n: wb * 7 * T_CORE + m * T_CORE + n * NT
            tv_au = V(lambda e, n=n: e.tensor_copy(au_sb[:, :], Pb[n][:, :]),
                      [t_au] + scratch_free + tmpr_free[wb])
            tv_aubf = V(lambda e, c=col(0), n=n: e.tensor_copy(
                tmpr[:, c:c + NT], Pb[n][:, :]))
            tv_delta = V(lambda e, s=s, n=n: e.tensor_scalar(
                delta_sb[:, :], Pb[2 + n][:, :], lp_sb[:, s:s + 1], None,
                ALU.subtract), [t_gp])
            bank_free[n] = [tv_aubf]
            bank_free[2 + n] = [tv_delta]
            ts_abs = S(lambda e: e.activation(absd_sb[:, :], delta_sb[:, :],
                                              AF.Abs), [tv_delta])
            ts_log = S(lambda e: e.activation(logd_sb[:, :], absd_sb[:, :],
                                              AF.Ln))
            tv_aud = V(lambda e: e.tensor_tensor(aud_sb[:, :], delta_sb[:, :],
                                                 au_sb[:, :], ALU.mult))
            tv_keep = V(lambda e: e.tensor_scalar(keep_sb[:, :], absd_sb[:, :],
                                                  THD, None, ALU.is_le),
                        [ts_abs])
            last_v = tv_keep
            for m in range(1, 7):
                odd = (m % 2 == 1)
                scl = float(m - 1) if odd else float(m)
                ts_dt = S(lambda e, scl=scl, m=m: e.activation(
                    dt_sb[:, :], logd_sb[:, :], AF.Exp,
                    bias=ndf_sb[:, m - 1:m], scale=scl),
                    [ts_log, last_v])
                src_dt = dt_sb
                tok_dt = ts_dt
                if m >= 5:
                    tok_dt = V(lambda e: e.tensor_tensor(
                        dtk_sb[:, :], dt_sb[:, :], keep_sb[:, :], ALU.mult),
                        [ts_dt, tv_keep])
                    src_dt = dtk_sb
                other = aud_sb if odd else au_sb
                last_v = V(lambda e, c=col(m), sd=src_dt, o=other:
                           e.tensor_tensor(tmpr[:, c:c + NT], sd[:, :],
                                           o[:, :], ALU.mult),
                           [tok_dt, tv_aud])
            scratch_free = [last_v]
        dst = sc_tmp[:, s].rearrange("m p t -> p m t")
        src = tmpr[:, wb * 7 * T_CORE:(wb + 1) * 7 * T_CORE].rearrange(
            "p (m t) -> p m t", m=7)
        P.dma("sync", dst, src, [scratch_free[0]], s_o[wb])
        tmpr_free[wb] = [(s_o[wb], s_o[wb].n)]
    tmp_spill_done = [(s_o[0], s_o[0].n), (s_o[1], s_o[1].n)]
    wbf_carry = wb_free[0] + wb_free[1]
    P.barrier(s_bar, DMA_SEMS, {"tensor": s_t, "vector": s_v, "scalar": s_s})

    ph_close()
    BIG = stack.enter_context(
        nc.sbuf_tensor("BIG", [128, 28 * T_CORE], BF16, side="right"))

    ph_open()
    WB3 = W_glob
    silu_sb = sbr("silu_sb", [128, 2 * 512], F32)
    a1in = sbr("a1in", [128, T_CORE], F32)
    a1out = sbr("a1out", [128, 2 * T_CORE], F32)
    part_in = [None, sc_a1A, sc_a1B]
    part_out = [sc_a1A, sc_a1B, sc_a1]
    for part in range(3):
        fbase = part * FC_P
        wb_free3 = [list(wbf_carry), list(wbf_carry)]
        silu_free = [[], [], [], []]
        big_toks = []
        for f in range(FC_P):
            wb = f % 2
            fa = fbase + f
            P.dma("gpsimd", WB3[:, wb * 8192: wb * 8192 + 4096],
                  w_fg_p[fa], wb_free3[wb], s_w[wb])
            P.dma("gpsimd", WB3[:, wb * 8192 + 4096: wb * 8192 + 8192],
                  w_fu_p[fa], [], s_w[wb])
            wtok = (s_w[wb], s_w[wb].n)
            mm_t = []
            for n in range(NHALF):
                pg = [(WB3[:, wb * 8192 + hc * 128: wb * 8192 + (hc + 1) * 128],
                       xl(hc, n)) for hc in range(HC)]
                tg = mm_group(Pb[n][:, :], pg, [wtok], n)
                pu = [(WB3[:, wb * 8192 + 4096 + hc * 128:
                           wb * 8192 + 4096 + (hc + 1) * 128],
                       xl(hc, n)) for hc in range(HC)]
                tu = mm_group(Pb[2 + n][:, :], pu, [wtok], 2 + n)
                mm_t.append((tg, tu))
            wb_free3[wb] = [mm_t[1][0], mm_t[1][1]]
            for n in range(NHALF):
                tg, tu = mm_t[n]
                sb = n
                ts_si = S(lambda e, sb=sb, n=n: e.activation(
                    silu_sb[:, sb * 512:(sb + 1) * 512], Pb[n][:, :],
                    AF.Silu), [tg] + silu_free[sb])
                bank_free[n] = [ts_si]
                col = f * T_CORE + n * NT
                tv = V(lambda e, c=col, sb=sb, n=n: e.tensor_tensor(
                    BIG[:, c:c + NT], silu_sb[:, sb * 512:(sb + 1) * 512],
                    Pb[2 + n][:, :], ALU.mult), [ts_si, tu])
                bank_free[2 + n] = [tv]
                silu_free[sb] = [tv]
                big_toks.append(tv)
        big_ready = big_toks[-1]
        wb_free4 = [[big_ready], [big_ready]]
        a1in_free = []
        a1out_free = [[], []]
        for h in range(HC):
            wb = h % 2
            P.dma("gpsimd", WB3[:, wb * 8192: wb * 8192 + FC_P * 128],
                  w_fd_p[h, part], wb_free4[wb], s_w[wb])
            wtok = (s_w[wb], s_w[wb].n)
            if part > 0:
                P.dma("sync", a1in[:, :],
                      part_in[part][h], a1in_free, s_i[wb])
                a1tok = (s_i[wb], s_i[wb].n)
            ev_toks = []
            for n in range(NHALF):
                pbi = 2 * wb + n
                pairs = [(WB3[:, wb * 8192 + fc * 128: wb * 8192 + (fc + 1) * 128],
                          BIG[:, fc * T_CORE + n * NT: fc * T_CORE + (n + 1) * NT])
                         for fc in range(FC_P)]
                tm = mm_group(Pb[pbi][:, :], pairs,
                              [wtok, big_ready], pbi)
                if part == 0:
                    tv = V(lambda e, pbi=pbi, wb=wb, n=n: e.tensor_copy(
                        a1out[:, wb * T_CORE + n * NT: wb * T_CORE + (n + 1) * NT],
                        Pb[pbi][:, :]), [tm] + a1out_free[wb])
                else:
                    tv = V(lambda e, pbi=pbi, wb=wb, n=n: e.tensor_tensor(
                        a1out[:, wb * T_CORE + n * NT: wb * T_CORE + (n + 1) * NT],
                        Pb[pbi][:, :],
                        a1in[:, n * NT:(n + 1) * NT],
                        ALU.add), [tm, a1tok] + a1out_free[wb])
                bank_free[pbi] = [tv]
                ev_toks.append(tv)
            wb_free4[wb] = [ev_toks[-1]]
            if part > 0:
                a1in_free = [ev_toks[-1]]
            P.dma("sync", part_out[part][h],
                  a1out[:, wb * T_CORE:(wb + 1) * T_CORE], ev_toks, s_o[wb])
            a1out_free[wb] = [(s_o[wb], s_o[wb].n)]
        wbf_carry = wb_free4[0] + wb_free4[1]
        P.barrier(s_bar, DMA_SEMS, {"tensor": s_t, "vector": s_v, "scalar": s_s})

    ph_close()
    ph_open()
    WB5 = W_glob
    a1in5 = sbr("a1in5", [128, T_CORE], F32)
    a_ring = sbr("a_ring", [128, 2 * T_CORE], F32)
    sq5 = sbr("sq5", [128, 2 * 512], BF16)
    ab5 = sbr("ab5", [128, 2 * 512], BF16)
    rtm5 = W_glob[:, 8192:10240].bitcast(F32)
    sc_tmp_flat = sc_tmp.rearrange("m s p t -> (m s) p t")
    pass_in5 = [sc_a1, sc_a1A]
    pass_out5 = [sc_a1A, sc_a]
    last_stats = None
    for qp in range(2):
        qbase = qp * 28
        prev_mm_tok = [] if qp == 0 else [big5_last_mm]
        for blk in range(4):
            s0_ = qbase + blk * 7
            src = sc_tmp_flat[s0_:s0_ + 7].rearrange("c p t -> p c t")
            dst = BIG[:, blk * 7 * T_CORE:(blk + 1) * 7 * T_CORE].rearrange(
                "p (c t) -> p c t", c=7)
            P.dma("sync", dst, src,
                  (tmp_spill_done if (qp == 0 and blk == 0) else []) +
                  prev_mm_tok, s_j[blk % 2])
        tmp_rd = (s_j[0], s_j[0].n)
        tmp_rd2 = (s_j[1], s_j[1].n)
        wb_free5 = [list(wbf_carry), list(wbf_carry)]
        a1in_free5 = []
        rtm_free = []
        a_free = [[], []]
        sq_free5 = [[], []]
        pending5 = [None, None]
        stats_guard5 = [[], []]
        for h in range(HC):
            wb = h % 2
            src_w = w_cat_p[h][:, qbase * 128:(qbase + 28) * 128]
            P.dma("gpsimd", WB5[:, wb * 3584:(wb + 1) * 3584], src_w,
                  wb_free5[wb], s_w[wb])
            wtok = (s_w[wb], s_w[wb].n)
            P.dma("sync", a1in5[:, :],
                  pass_in5[qp][h], a1in_free5, s_i[wb])
            a1tok = (s_i[wb], s_i[wb].n)
            ev_toks = []
            for n in range(NHALF):
                pbi = 2 * wb + n
                pairs = [(WB5[:, wb * 3584 + q * 128: wb * 3584 + (q + 1) * 128],
                          BIG[:, q * T_CORE + n * NT: q * T_CORE + (n + 1) * NT])
                         for q in range(28)]
                tm = mm_group(Pb[pbi][:, :], pairs, [wtok, tmp_rd, tmp_rd2], pbi)
                big5_last_mm = tm
                if qp == 1 and pending5[n] is not None:
                    entries, swaits, pwb = pending5[n]
                    last_stats = stats_mms(entries, swaits)
                    sq_free5[n] = [last_stats]
                    stats_guard5[pwb] = [last_stats]
                    pending5[n] = None
                tv = V(lambda e, pbi=pbi, wb=wb, n=n: e.tensor_tensor(
                    a_ring[:, wb * T_CORE + n * NT: wb * T_CORE + (n + 1) * NT],
                    Pb[pbi][:, :],
                    a1in5[:, n * NT:(n + 1) * NT],
                    ALU.add), [tm, a1tok] + a_free[wb] + stats_guard5[wb])
                bank_free[pbi] = [tv]
                if qp == 1:
                    tab = V(lambda e, wb=wb, n=n: e.tensor_copy(
                        ab5[:, n * 512:(n + 1) * 512],
                        a_ring[:, wb * T_CORE + n * NT: wb * T_CORE + (n + 1) * NT]),
                        [tv] + sq_free5[n])
                    tsq = S(lambda e, wb=wb, n=n: e.activation(
                        sq5[:, n * 512:(n + 1) * 512],
                        a_ring[:, wb * T_CORE + n * NT: wb * T_CORE + (n + 1) * NT],
                        AF.Square), [tv] + sq_free5[n])
                    pending5[n] = (
                        [(st0, n * NT, (n + 1) * NT,
                          ab5[:, n * 512:(n + 1) * 512]),
                         (st1, n * NT, (n + 1) * NT,
                          sq5[:, n * 512:(n + 1) * 512])], [tab, tsq], wb)
                ev_toks.append(tv)
            wb_free5[wb] = [ev_toks[-1]]
            a1in_free5 = [ev_toks[-1]]
            if qp == 0:
                P.dma("sync", pass_out5[qp][h],
                      a_ring[:, wb * T_CORE:(wb + 1) * T_CORE], ev_toks,
                      s_o[wb])
                a_free[wb] = [(s_o[wb], s_o[wb].n)]
            else:
                tm_evs = []
                for rnd in range(2):
                    pbt = 2 * wb + rnd
                    def tpa(eng, wb=wb, rnd=rnd, pbt=pbt):
                        inst = None
                        for piece in range(4):
                            tt_ = 4 * rnd + piece
                            inst = eng.transpose(
                                Pb[pbt][:, piece * 128:(piece + 1) * 128],
                                a_ring[:, wb * T_CORE + tt_ * 128:
                                       wb * T_CORE + (tt_ + 1) * 128],
                                ident[:, :])
                        return inst
                    P.op("tensor", tpa, ev_toks + bank_free[pbt] + rtm_free,
                         inc=s_t)
                    t_tp = (s_t, s_t.n)
                    tm_ev = V(lambda e, rnd=rnd, pbt=pbt: e.tensor_copy(
                        rtm5[:, rnd * 512:(rnd + 1) * 512], Pb[pbt][:, :]),
                        [t_tp])
                    bank_free[pbt] = [tm_ev]
                    tm_evs.append(tm_ev)
                dst_tm = sc_a[:, :, h * 128:(h + 1) * 128].rearrange(
                    "c p t -> p c t")
                P.dma("sync", dst_tm,
                      rtm5[:, :].rearrange("p (c t) -> p c t", c=8),
                      tm_evs, s_o[wb])
                rtm_free = [(s_o[wb], s_o[wb].n)]
                a_free[wb] = tm_evs + [(s_o[wb], s_o[wb].n)]
        if qp == 1:
            for n in range(NHALF):
                if pending5[n] is not None:
                    entries, swaits, pwb = pending5[n]
                    last_stats = stats_mms(entries, swaits)
                    pending5[n] = None
        wbf_carry = wb_free5[0] + wb_free5[1]
        P.barrier(s_bar, DMA_SEMS, {"tensor": s_t, "vector": s_v, "scalar": s_s})
    lna_tok = ln_cols(0, last_stats, D)
    st0.reset()
    st1.reset()
    P.barrier(s_bar, DMA_SEMS, {"tensor": s_t, "vector": s_v, "scalar": s_s})

    ph_close()
    stack.close()

    ph_open()
    xraw6 = sbr("xraw6", [128, D], F32)
    a_fm = sbr("a_fm", [128, D], F32)
    ev6r = sbr("ev6r", [128, 2 * 512], F32)
    x2t = sbr("x2t", [128, 2 * D], F32)
    xlnb6 = sbr("xlnb6", [128, 2048], BF16)
    stc6 = sbr("stc6", [128, 20], F32)
    x_free6 = []
    a_free6 = []
    atm_free = [[], []]
    x2t_free = [[], []]
    xlnb_free = []
    stc6_free = [[], []]
    for tt in range(8):
        b = tt % 2
        o = 9 * b
        P.dma("sync", a_fm[:, :], sc_a[tt],
              a_free6, s_j[b])
        lda = (s_j[b], s_j[b].n)
        P.dma("sync", xraw6[:, :], x_in[tt * 128:(tt + 1) * 128, :],
              x_free6, s_i[0])
        ldx = (s_i[0], s_i[0].n)
        ev_toks6 = []
        for g in range(8):
            ev = S(lambda e, b=b, g=g, tt=tt: e.activation(
                ev6r[:, (g % 2) * 512:(g % 2 + 1) * 512],
                a_fm[:, g * 512:(g + 1) * 512],
                AF.Identity, bias=lncol[:, 8 + tt:9 + tt],
                scale=lncol[:, tt:tt + 1]),
                [lda, lna_tok] + atm_free[g % 2])
            tadd = V(lambda e, b=b, g=g: e.tensor_tensor(
                x2t[:, b * D + g * 512: b * D + (g + 1) * 512],
                ev6r[:, (g % 2) * 512:(g % 2 + 1) * 512],
                xraw6[:, g * 512:(g + 1) * 512], ALU.add),
                [ev, ldx] + x2t_free[b])
            atm_free[g % 2] = [tadd]
            ev_toks6.append(tadd)
        t_x2 = ev_toks6[-1]
        tsum = V(lambda e, b=b, o=o: e.reduce_sum(
            stc6[:, o:o + 1], x2t[:, b * D:(b + 1) * D],
            mybir.AxisListType.X), stc6_free[b])
        tsq = S(lambda e, b=b, o=o: e.activation(
            xraw6[:, :], x2t[:, b * D:(b + 1) * D], AF.Square,
            accum_out=stc6[:, o + 1:o + 2]), [t_x2])
        a_free6 = [ev_toks6[-1]]
        x_free6 = [t_x2, tsq]
        V(lambda e, o=o: e.tensor_scalar(stc6[:, o + 2:o + 3], stc6[:, o:o + 1],
                                         1.0 / D, None, ALU.mult), [tsum, tsq])
        V(lambda e, o=o: e.tensor_scalar(stc6[:, o + 3:o + 4],
                                         stc6[:, o + 1:o + 2], 1.0 / D, None,
                                         ALU.mult))
        V(lambda e, o=o: e.tensor_tensor(stc6[:, o + 4:o + 5],
                                         stc6[:, o + 2:o + 3],
                                         stc6[:, o + 2:o + 3], ALU.mult))
        V(lambda e, o=o: e.scalar_tensor_tensor(
            stc6[:, o + 5:o + 6], stc6[:, o + 3:o + 4], EPS,
            stc6[:, o + 4:o + 5], ALU.add, ALU.subtract))
        tv6 = V(lambda e, o=o: e.reciprocal(stc6[:, o + 6:o + 7],
                                            stc6[:, o + 5:o + 6]))
        ts6 = S(lambda e, o=o, tt=tt: e.activation(
            lncol[:, 16 + tt:17 + tt], stc6[:, o + 6:o + 7], AF.Sqrt), [tv6])
        tmb6 = V(lambda e, o=o, tt=tt: e.scalar_tensor_tensor(
            lncol[:, 24 + tt:25 + tt], stc6[:, o + 2:o + 3], -1.0,
            lncol[:, 16 + tt:17 + tt], ALU.mult, ALU.mult), [ts6])
        stc6_free[b] = [tmb6]
        spill_toks = []
        last_tpx = None
        for hh in range(2):
            t_xlb = V(lambda e, b=b, hh=hh, tt=tt: e.tensor_scalar(
                xlnb6[:, :], x2t[:, b * D + hh * 2048: b * D + (hh + 1) * 2048],
                lncol[:, 16 + tt:17 + tt],
                lncol[:, 24 + tt:25 + tt], ALU.mult, ALU.add),
                [tmb6] + xlnb_free)
            for g in range(4 * hh, 4 * hh + 4):
                pbi = 2 + g % 2
                def tpx(eng, g=g, hh=hh, pbi=pbi):
                    inst = None
                    pbb = Pb[pbi][:, :].bitcast(BF16)
                    for jj in range(4):
                        c_loc = (4 * g + jj) - hh * 16
                        inst = eng.transpose(pbb[:, jj * 128:(jj + 1) * 128],
                                             xlnb6[:, c_loc * 128:
                                                   (c_loc + 1) * 128],
                                             identb[:, :])
                    return inst
                P.op("tensor", tpx, [t_xlb] + bank_free[pbi], inc=s_t)
                tpt = (s_t, s_t.n)
                last_tpx = tpt
                n_ = tt // 4
                j_ = tt % 4
                dst = XL[:, :].rearrange("p (c t) -> p c t", c=HC)[
                    :, 4 * g:4 * g + 4,
                    n_ * NT + j_ * 128: n_ * NT + (j_ + 1) * 128]
                evx = S(lambda e, dst=dst, pbi=pbi: e.activation(
                    dst, Pb[pbi][:, :].bitcast(BF16)[:, 0:512].rearrange(
                        "p (c t) -> p c t", c=4), AF.Copy), [tpt])
                bank_free[pbi] = [evx]
            xlnb_free = [last_tpx]
        P.dma("sync", sc_x2[tt], x2t[:, b * D:(b + 1) * D],
              [t_x2, tsq, tsum, last_tpx], s_o[b])
        x2t_free[b] = [(s_o[b], s_o[b].n)]
    P.barrier(s_bar, DMA_SEMS, {"tensor": s_t, "vector": s_v, "scalar": s_s})
    ph_close()

    ph_open()
    WB8 = W_glob
    lg_fm = sbr("lg_fm", [8, T_CORE], F32)
    h_sb = sbr("h_sb", [128, T_CORE], F32)
    lg_tm = sbr("lg_tm", [128, 64], F32)
    m1_sb = sbr("m1_sb", [128, 8], F32)
    m2_sb = sbr("m2_sb", [128, 8], F32)
    d1_sb = sbr("d1_sb", [128, 64], F32)
    d2_sb = sbr("d2_sb", [128, 64], F32)
    msk_sb = sbr("msk_sb", [128, 64], F32)
    e2_sb = sbr("e2_sb", [128, 8], F32)
    den_sb = sbr("den_sb", [128, 8], F32)
    w1_sb = sbr("w1_sb", [128, 8], F32)
    w2_sb = sbr("w2_sb", [128, 8], F32)
    wtm_sb = sbr("wtm_sb", [128, 64], F32)
    wfm_sb = sbr("wfm_sb", [8, T_CORE], F32)
    hw_sb = sbr("hw_sb", [128, T_CORE], BF16)
    f_ring = sbr("f_ring", [128, 2 * T_CORE], F32)
    sq8 = sbr("sq8", [128, 2 * 512], BF16)
    fb8 = sbr("fb8", [128, 2 * 512], BF16)
    P.dma("gpsimd", asb8[:, :].rearrange("p (c f) -> p c f", c=HC),
          a_T.rearrange("(c p) f -> p c f", p=128), [], s_k2)
    P.dma("gpsimd", wr8[:, :].rearrange("p (c f) -> p c f", c=HC),
          w_rT.rearrange("(c p) f -> p c f", p=128), [], s_k2)
    P.dma("gpsimd", bfl8[:, :], b_fl, [], s_k2)
    pre = (s_k2, s_k2.n)
    for n in range(NHALF):
        pairs = [(wr8[:, hc * 8:(hc + 1) * 8], xl(hc, n)) for hc in range(HC)]
        tl = mm_group(Pb[n][0:8, :], pairs, [pre], n)
        tvl = V(lambda e, n=n: e.tensor_copy(lg_fm[:, n * NT:(n + 1) * NT],
                                             Pb[n][0:8, :]), [tl])
        bank_free[n] = [tvl]
        pairs_h = [(asb8[:, hc * 128:(hc + 1) * 128], xl(hc, n))
                   for hc in range(HC)]
        th = mm_group(Pb[2 + n][:, :], pairs_h, [pre], 2 + n)
        tvh = V(lambda e, n=n: e.tensor_copy(h_sb[:, n * NT:(n + 1) * NT],
                                             Pb[2 + n][:, :]), [th])
        bank_free[2 + n] = [tvh]
    def tplg(eng):
        inst = None
        for tt in range(8):
            inst = eng.transpose(Pb[0][:, tt * 8:(tt + 1) * 8],
                                 lg_fm[0:8, tt * 128:(tt + 1) * 128],
                                 ident[0:8, 0:8])
        return inst
    P.op("tensor", tplg, [tvl] + bank_free[0], inc=s_t)
    t_lgt = (s_t, s_t.n)
    tv = V(lambda e: e.tensor_copy(lg_tm[:, :], Pb[0][:, 0:64]), [t_lgt])
    bank_free[0] = [tv]
    lg3 = lambda ap: ap.rearrange("p (t e) -> p t e", e=8)
    V(lambda e: e.tensor_reduce(m1_sb[:, :], lg3(lg_tm[:, :]),
                                mybir.AxisListType.X, ALU.max))
    m1b = m1_sb[:, :].unsqueeze(2).broadcast_to([128, 8, 8])
    V(lambda e: e.tensor_tensor(lg3(d1_sb[:, :]), lg3(lg_tm[:, :]), m1b,
                                ALU.is_equal))
    V(lambda e: e.scalar_tensor_tensor(msk_sb[:, :], d1_sb[:, :], -1e30,
                                       lg_tm[:, :], ALU.mult, ALU.add))
    V(lambda e: e.tensor_reduce(m2_sb[:, :], lg3(msk_sb[:, :]),
                                mybir.AxisListType.X, ALU.max))
    tvm = V(lambda e: e.tensor_tensor(den_sb[:, :], m2_sb[:, :],
                                      m1_sb[:, :], ALU.subtract))
    tse = S(lambda e: e.activation(e2_sb[:, :], den_sb[:, :], AF.Exp),
            [tvm])
    V(lambda e: e.tensor_scalar(den_sb[:, :], e2_sb[:, :], 1.0, None,
                                ALU.add), [tse])
    V(lambda e: e.reciprocal(w1_sb[:, :], den_sb[:, :]))
    tvw = V(lambda e: e.tensor_tensor(w2_sb[:, :], w1_sb[:, :], e2_sb[:, :],
                                      ALU.mult))
    m2b = m2_sb[:, :].unsqueeze(2).broadcast_to([128, 8, 8])
    V(lambda e: e.tensor_tensor(lg3(d2_sb[:, :]), lg3(msk_sb[:, :]), m2b,
                                ALU.is_equal))
    w1b = w1_sb[:, :].unsqueeze(2).broadcast_to([128, 8, 8])
    w2b = w2_sb[:, :].unsqueeze(2).broadcast_to([128, 8, 8])
    V(lambda e: e.tensor_tensor(lg3(wtm_sb[:, :]), lg3(d1_sb[:, :]), w1b,
                                ALU.mult), [tvw])
    V(lambda e: e.tensor_tensor(lg3(d2_sb[:, :]), lg3(d2_sb[:, :]), w2b,
                                ALU.mult))
    tv_wtm = V(lambda e: e.tensor_tensor(wtm_sb[:, :], wtm_sb[:, :],
                                         d2_sb[:, :], ALU.add))
    def tpw(eng):
        inst = None
        for tt in range(8):
            dst = Pb[tt // 4][0:8, (tt % 4) * 128:(tt % 4 + 1) * 128]
            inst = eng.transpose(dst, wtm_sb[:, tt * 8:(tt + 1) * 8],
                                 ident[:, :])
        return inst
    P.op("tensor", tpw, [tv_wtm] + bank_free[0] + bank_free[1], inc=s_t)
    t_wfm = (s_t, s_t.n)
    tv_wfm0 = V(lambda e: e.tensor_copy(wfm_sb[0:8, 0:512], Pb[0][0:8, :]),
                [t_wfm])
    tv_wfm = V(lambda e: e.tensor_copy(wfm_sb[0:8, 512:1024], Pb[1][0:8, :]))
    bank_free[0] = [tv_wfm0]
    bank_free[1] = [tv_wfm]
    for n in range(NHALF):
        tsel = mm_group(Pb[2 + n][:, :],
                        [(esel_sb[:, :], wfm_sb[0:8, n * NT:(n + 1) * NT])],
                        [tv_wfm], 2 + n)
        thw = V(lambda e, n=n: e.tensor_tensor(
            hw_sb[:, n * NT:(n + 1) * NT], h_sb[:, n * NT:(n + 1) * NT],
            Pb[2 + n][:, :], ALU.mult), [tsel, tvh])
        bank_free[2 + n] = [thw]
    hw_tok = (s_v, s_v.n)
    rtm8 = W_glob[:, 8192:10240].bitcast(F32)
    rtm_free8 = []
    wb_free8 = [list(wbf_carry), list(wbf_carry)]
    f_free = [[], []]
    sq_free8 = [[], []]
    pending8 = [None, None]
    stats_guard8 = [[], []]
    last_stats8 = None
    for h in range(HC):
        wb = h % 2
        P.dma("gpsimd", WB8[:, wb * 4096:(wb + 1) * 4096], w_b_p[h],
              wb_free8[wb], s_w[wb])
        wtok = (s_w[wb], s_w[wb].n)
        ev_toks = []
        for n in range(NHALF):
            pbi = 2 * wb + n
            pairs = [(WB8[:, wb * 4096 + hc * 128: wb * 4096 + (hc + 1) * 128],
                      xl(hc, n)) for hc in range(HC)]
            pairs.append((bfl8[:, h * 128:(h + 1) * 128],
                          hw_sb[:, n * NT:(n + 1) * NT]))
            tm = mm_group(Pb[pbi][:, :], pairs, [wtok, hw_tok, pre], pbi)
            if pending8[n] is not None:
                entries, swaits, pwb = pending8[n]
                last_stats8 = stats_mms(entries, swaits)
                sq_free8[n] = [last_stats8]
                stats_guard8[pwb] = [last_stats8]
                pending8[n] = None
            tv = V(lambda e, pbi=pbi, wb=wb, n=n, h=h: e.tensor_scalar(
                f_ring[:, wb * T_CORE + n * NT: wb * T_CORE + (n + 1) * NT],
                Pb[pbi][:, :], bb_sb[:, h:h + 1], None, ALU.add),
                [tm] + f_free[wb] + stats_guard8[wb])
            bank_free[pbi] = [tv]
            tfb = V(lambda e, wb=wb, n=n: e.tensor_copy(
                fb8[:, n * 512:(n + 1) * 512],
                f_ring[:, wb * T_CORE + n * NT: wb * T_CORE + (n + 1) * NT]),
                [tv] + sq_free8[n])
            tsq = S(lambda e, wb=wb, n=n: e.activation(
                sq8[:, n * 512:(n + 1) * 512],
                f_ring[:, wb * T_CORE + n * NT: wb * T_CORE + (n + 1) * NT],
                AF.Square), [tv] + sq_free8[n])
            pending8[n] = (
                [(st0, n * NT, (n + 1) * NT,
                  fb8[:, n * 512:(n + 1) * 512]),
                 (st1, n * NT, (n + 1) * NT,
                  sq8[:, n * 512:(n + 1) * 512])], [tfb, tsq], wb)
            ev_toks.append(tv)
        wb_free8[wb] = [ev_toks[-1]]
        tm_evs8 = []
        for rnd in range(2):
            pbt = 2 * wb + rnd
            def tpf(eng, wb=wb, rnd=rnd, pbt=pbt):
                inst = None
                for piece in range(4):
                    tt_ = 4 * rnd + piece
                    inst = eng.transpose(
                        Pb[pbt][:, piece * 128:(piece + 1) * 128],
                        f_ring[:, wb * T_CORE + tt_ * 128:
                               wb * T_CORE + (tt_ + 1) * 128],
                        ident[:, :])
                return inst
            P.op("tensor", tpf, ev_toks + bank_free[pbt] + rtm_free8,
                 inc=s_t)
            t_tp = (s_t, s_t.n)
            tm_ev = V(lambda e, rnd=rnd, pbt=pbt: e.tensor_copy(
                rtm8[:, rnd * 512:(rnd + 1) * 512], Pb[pbt][:, :]), [t_tp])
            bank_free[pbt] = [tm_ev]
            tm_evs8.append(tm_ev)
        dst_tm = sc_f[:, :, h * 128:(h + 1) * 128].rearrange("c p t -> p c t")
        P.dma("sync", dst_tm, rtm8[:, :].rearrange("p (c t) -> p c t", c=8),
              tm_evs8, s_o[wb])
        rtm_free8 = [(s_o[wb], s_o[wb].n)]
        f_free[wb] = tm_evs8 + [(s_o[wb], s_o[wb].n)]
    for n in range(NHALF):
        if pending8[n] is not None:
            entries, swaits, pwb = pending8[n]
            last_stats8 = stats_mms(entries, swaits)
            pending8[n] = None
    ln3_tok = ln_cols(32, last_stats8, D)
    st0.reset()
    st1.reset()
    P.barrier(s_bar, DMA_SEMS, {"tensor": s_t, "vector": s_v, "scalar": s_s})

    ph_close()
    p_stack.close()
    w_stack.close()
    ph_open()
    x2t9 = sbr("x2t9", [128, 2 * D], F32)
    ft9 = sbr("ft9", [128, 2 * D], F32)
    lnf = sbr("lnf", [128, D], F32)
    oring = sbr("oring", [128, 2 * D], F32)
    x2_free9 = [[], []]
    f_free9 = [[], []]
    lnf_free = []
    o_free9 = [[], []]
    for tt in range(8):
        b = tt % 2
        P.dma("sync", x2t9[:, b * D:(b + 1) * D], sc_x2[tt],
              x2_free9[b], s_i[b])
        ldx2 = (s_i[b], s_i[b].n)
        P.dma("sync", ft9[:, b * D:(b + 1) * D], sc_f[tt],
              f_free9[b], s_j[b])
        ldf = (s_j[b], s_j[b].n)
        t_ln = S(lambda e, b=b, tt=tt: e.activation(
            lnf[:, :], ft9[:, b * D:(b + 1) * D],
            AF.Identity, bias=lncol[:, 40 + tt:41 + tt],
            scale=lncol[:, 32 + tt:33 + tt]),
            [ldf, ln3_tok] + lnf_free)
        t_out = V(lambda e, b=b: e.tensor_tensor(
            oring[:, b * D:(b + 1) * D], lnf[:, :],
            x2t9[:, b * D:(b + 1) * D], ALU.add),
            [ldx2, t_ln] + o_free9[b])
        lnf_free = [t_out]
        x2_free9[b] = [t_out]
        f_free9[b] = [t_ln]
        P.dma("sync", out_hbm[tt * 128:(tt + 1) * 128, :],
              oring[:, b * D:(b + 1) * D], [t_out], s_o[b])
        o_free9[b] = [(s_o[b], s_o[b].n)]
    ph_close()
    P.op("sync", lambda e: e.nop(), [(s, s.n) for s in ALL_DMA_SEMS])
    P.op("gpsimd", lambda e: e.nop(), [(s_o[0], s_o[0].n), (s_o[1], s_o[1].n)])

    P.emit(cut=_CUT[0] if _CUT else None)
    return nc



def _prep_host(inputs):
    c = np.ascontiguousarray
    up_w = inputs["L0_attn_up_w"]
    gate_w = inputs["L0_attn_gate_w"]
    lp = inputs["L0_attn_local_point"]
    lao = inputs["L0_attn_local_approx_output"]
    fuse = inputs["L0_attn_fuse_weight"]
    df = inputs["L0_attn_discount_factor"]
    fg = inputs["L0_attn_ffn_gate_w"]
    fu = inputs["L0_attn_ffn_up_w"]
    fd = inputs["L0_attn_ffn_down_w"]
    wb = inputs["L0_ffn_W_base"]
    bb = inputs["L0_ffn_b_base"]
    wr = inputs["L0_ffn_W_router"]
    A = inputs["L0_ffn_A"]
    B = inputs["L0_ffn_B"]
    f32 = np.float32
    def tile_kM(wT, steps):
        kt, mt = wT.shape
        return c(wT.reshape(kt // 128, 128, steps, mt // steps)
                 .transpose(2, 1, 0, 3).reshape(steps, 128, -1))

    w_catT = np.concatenate(
        [np.asarray(lao, f32).T] +
        [np.asarray(fuse[:, :, m], f32).T for m in range(6)], axis=0)
    fdT = np.asarray(fd, f32).T
    shared = {
        "w_up_p": tile_kM(np.asarray(up_w, f32).T, SC),
        "w_gate_p": tile_kM(np.asarray(gate_w, f32).T, SC),
        "lp_pc": c(np.asarray(lp, f32).reshape(SC, 128).T),
        "df_bc": c(np.broadcast_to(np.asarray(df, f32), (128, 6))),
        "w_cat_p": tile_kM(w_catT, HC),
        "w_fg_p": tile_kM(np.asarray(fg, f32).T, 78),
        "w_fu_p": tile_kM(np.asarray(fu, f32).T, 78),
        "w_fd_p": c(fdT.reshape(3, FC_P, 128, HC, 128)
                    .transpose(3, 0, 2, 1, 4).reshape(HC, 3, 128, FC_P * 128)),
        "w_b_p": tile_kM(np.asarray(wb, f32).T, HC),
        "bb_pc": c(np.asarray(bb, f32).reshape(HC, 128).T),
        "w_rT": c(np.asarray(wr, f32).T),
        "a_T": c(np.asarray(A, f32).reshape(128, D).T),
        "b_fl": c(np.asarray(B, f32).transpose(0, 2, 1).reshape(128, D)),
    }
    return shared


def kernel(**inputs):
    x = np.asarray(inputs["L0_attn_hidden_states"], np.float32).reshape(8192, D)
    shared = _prep_host(inputs)
    nc = build_core_program(_DEBUG_DUMP)
    in_maps = []
    for cid in range(8):
        m = dict(shared)
        m["x_in"] = np.ascontiguousarray(x[cid * T_CORE:(cid + 1) * T_CORE])
        in_maps.append(m)
    res = run_bass_kernel_spmd(nc, in_maps, core_ids=list(range(8)))
    out = np.concatenate([res.results[cid]["out"] for cid in range(8)], axis=0)
    if _DEBUG_DUMP:
        kernel._dbg = res.results
    return out.reshape(1, 8192, D)


# version 12
# speedup vs baseline: 1.0473x; 1.0167x over previous
import math
import numpy as np
import concourse.bass as bass
import concourse.mybir as mybir
from concourse.bass_utils import run_bass_kernel_spmd

F32 = mybir.dt.float32
BF16 = mybir.dt.bfloat16
AF = mybir.ActivationFunctionType
ALU = mybir.AluOpType

ENGINES = ("tensor", "vector", "scalar", "sync", "gpsimd")

T_CORE = 1024
NHALF = 2
NT = 512
HC = 32
SC = 8
FC_P = 26
D = 4096
S_DIM = 1024
F_DIM = 9984
EPS = 1e-5
THD = 2.5

_DEBUG_DUMP = ()
_CUT = []


class Sem:
    __slots__ = ("h", "n", "name")

    def __init__(self, h, name):
        self.h = h
        self.n = 0
        self.name = name


class Prog:
    def __init__(self, nc):
        self.nc = nc
        self.q = {e: [] for e in ENGINES}
        self.sems = []
        self.cuts = []

    def sem(self, name):
        s = Sem(self.nc.alloc_semaphore(name), name)
        self.sems.append(s)
        return s

    def op(self, eng, fn, waits=(), inc=None, inc16=None):
        incs = []
        if inc is not None:
            inc.n += 1
            incs.append((inc, 1))
        if inc16 is not None:
            inc16.n += 16
            incs.append((inc16, 16))
        self.q[eng].append((tuple(waits), fn, tuple(incs)))

    def dma(self, eng, out_ap, in_ap, waits, sem):
        def fn(e, o=out_ap, i=in_ap):
            return e.dma_start(o, i)
        self.op(eng, fn, waits, inc16=sem)

    def barrier(self, sbar, dma_sems, eng_sems=()):
        eng_sems = dict(eng_sems)
        dma_waits = [(s, s.n) for s in dma_sems]
        for e in ENGINES:
            if e == "gpsimd":
                continue
            w = list(dma_waits) if e == "sync" else []
            if e in eng_sems:
                s = eng_sems[e]
                w.append((s, s.n))
            self.op(e, lambda eng: eng.nop(), w, inc=sbar)
        tgt = sbar.n
        for e in ENGINES:
            if e == "gpsimd":
                continue
            self.op(e, lambda eng: eng.nop(), [(sbar, tgt)])
        self.cuts.append({e: len(self.q[e]) for e in ENGINES})

    def emit(self, cut=None):
        if cut is not None:
            lim = self.cuts[cut]
            self.q = {e: self.q[e][:lim[e]] for e in ENGINES}
        def runner(items):
            def run(eng):
                for waits, fn, incs in items:
                    for (sem, val) in waits:
                        eng.wait_ge(sem.h, val)
                    inst = fn(eng)
                    for (sem, k) in incs:
                        inst = inst.then_inc(sem.h, k)
            return run

        with self.nc.Block() as block:
            block.tensor(runner(self.q["tensor"]))
            block.vector(runner(self.q["vector"]))
            block.scalar(runner(self.q["scalar"]))
            block.sync(runner(self.q["sync"]))
            block.gpsimd(runner(self.q["gpsimd"]))


class StatsBank:

    def __init__(self, ap, n_banks=2):
        self.ap = ap
        self.touched = [False] * n_banks

    def reset(self):
        for i in range(len(self.touched)):
            self.touched[i] = False

    def start_flag(self, col0, col1):
        b = col0 // 512
        assert col1 <= (b + 1) * 512
        st = not self.touched[b]
        self.touched[b] = True
        return st


def build_core_program(debug_dump=()):
    nc = bass.Bass("TRN2", target_bir_lowering=False)

    def din(name, shape):
        return nc.dram_tensor(name, shape, F32, kind="ExternalInput").ap()

    x_in = din("x_in", [T_CORE, D])
    w_up_p = din("w_up_p", [SC, 128, HC * 128])
    w_gate_p = din("w_gate_p", [SC, 128, HC * 128])
    lp_pc = din("lp_pc", [128, SC])
    df_bc = din("df_bc", [128, 6])
    w_cat_p = din("w_cat_p", [HC, 128, 56 * 128])
    w_fg_p = din("w_fg_p", [78, 128, HC * 128])
    w_fu_p = din("w_fu_p", [78, 128, HC * 128])
    w_fd_p = din("w_fd_p", [HC, 3, 128, FC_P * 128])
    w_b_p = din("w_b_p", [HC, 128, HC * 128])
    bb_pc = din("bb_pc", [128, HC])
    w_rT = din("w_rT", [D, 8])
    a_T = din("a_T", [D, 128])
    b_fl = din("b_fl", [128, D])

    out_hbm = nc.dram_tensor("out", [T_CORE, D], F32, kind="ExternalOutput").ap()

    def scratch(name, shape, dtype=F32):
        kind = "ExternalOutput" if name in debug_dump else "Internal"
        return nc.dram_tensor(name, shape, dtype, kind=kind).ap()

    sc_tmp = scratch("sc_tmp", [7, SC, 128, T_CORE], BF16)
    sc_a1A = scratch("sc_a1A", [HC, 128, T_CORE])
    sc_a1B = scratch("sc_a1B", [HC, 128, T_CORE])
    sc_a1 = scratch("sc_a1", [HC, 128, T_CORE])
    sc_a = scratch("sc_a", [8, 128, D])
    sc_x2 = scratch("sc_x2", [8, 128, D])
    sc_f = scratch("sc_f", [8, 128, D])

    ident_d = nc.inline_tensor(np.eye(128, dtype=np.float32), "ident")
    import ml_dtypes
    identb_d = nc.inline_tensor(np.eye(128).astype(ml_dtypes.bfloat16), "identb")
    ones_row_d = nc.inline_tensor(np.ones((1, 128), np.float32), "ones_row")
    neg_row_d = nc.inline_tensor(np.full((1, 128), -1.0, np.float32), "neg_row")
    ones_col_d = nc.inline_tensor(np.ones((128, 1), np.float32), "ones_col")
    ones_colb_d = nc.inline_tensor(np.ones((128, 1)).astype(ml_dtypes.bfloat16),
                                   "ones_colb")
    esel = np.zeros((8, 128), np.float32)
    for e in range(8):
        esel[e, 16 * e:16 * (e + 1)] = 2.0
    esel_d = nc.inline_tensor(esel, "esel")

    XL = nc.alloc_sbuf_tensor("XL", [128, HC * T_CORE], BF16, side="left")
    ident = nc.alloc_sbuf_tensor("ident_sb", [128, 128], F32, side="left")
    identb = nc.alloc_sbuf_tensor("identb_sb", [128, 128], BF16, side="left")
    lncol = nc.alloc_sbuf_tensor("lncol", [128, 48], F32, side="left")
    ones_row = nc.alloc_sbuf_tensor("ones_row_sb", [1, 128], F32, side="left")
    neg_row = nc.alloc_sbuf_tensor("neg_row_sb", [1, 128], F32, side="left")
    ones_col = nc.alloc_sbuf_tensor("ones_col_sb", [128, 1], F32, side="left")
    ones_colb = nc.alloc_sbuf_tensor("ones_colb_sb", [128, 1], BF16, side="left")
    esel_sb = nc.alloc_sbuf_tensor("esel_sb", [8, 128], F32, side="left")
    lp_sb = nc.alloc_sbuf_tensor("lp_sb", [128, SC], F32, side="left")
    ndf_sb = nc.alloc_sbuf_tensor("ndf_sb", [128, 6], F32, side="left")
    bb_sb = nc.alloc_sbuf_tensor("bb_sb", [128, HC], F32, side="left")
    r_mean = nc.alloc_sbuf_tensor("r_mean", [1, T_CORE], F32, side="left")
    r_tmp = nc.alloc_sbuf_tensor("r_tmp", [1, T_CORE], F32, side="left")
    r_aux = nc.alloc_sbuf_tensor("r_aux", [1, T_CORE], F32, side="left")
    r_mb = nc.alloc_sbuf_tensor("r_mb", [1, T_CORE], F32, side="left")
    bc_t = nc.alloc_sbuf_tensor("bc_t", [128, 512], F32, side="left")

    Pb = [nc.alloc_psum_tensor("P%d" % i, [128, 512], F32) for i in range(4)]
    S0 = nc.alloc_psum_tensor("S0", [1, T_CORE], F32)
    S1 = nc.alloc_psum_tensor("S1", [1, T_CORE], F32)
    st0 = StatsBank(S0)
    st1 = StatsBank(S1)

    P = Prog(nc)
    s_t = P.sem("s_t")
    s_v = P.sem("s_v")
    s_s = P.sem("s_s")
    s_bar = P.sem("s_bar")
    s_w = [P.sem("s_w0"), P.sem("s_w1")]
    s_i = [P.sem("s_i0"), P.sem("s_i1")]
    s_j = [P.sem("s_j0"), P.sem("s_j1")]
    s_o = [P.sem("s_o0"), P.sem("s_o1")]
    s_k = P.sem("s_k")
    s_k2 = P.sem("s_k2")
    DMA_SEMS = tuple(s_i + s_j + s_o + [s_k])
    ALL_DMA_SEMS = tuple(s_w + s_i + s_j + s_o + [s_k, s_k2])

    bank_free = {i: [] for i in range(4)}

    def TOP(fn, waits=(), inc=None):
        P.op("tensor", fn, waits, inc=inc)

    def mm_group(psum_ap, pairs, waits, bank, extra_first_start=True):
        w = list(waits) + bank_free[bank]

        def fn(eng, pairs=pairs, out=psum_ap):
            inst = None
            n = len(pairs)
            for i, (l, r) in enumerate(pairs):
                inst = eng.matmul(out, l, r,
                                  start=(extra_first_start and i == 0),
                                  stop=(i == n - 1), skip_group_check=True)
            return inst
        P.op("tensor", fn, w, inc=s_t)
        return (s_t, s_t.n)

    def stats_mms(entries, waits):
        plan = []
        for (sb, c0, c1, rhs) in entries:
            st = sb.start_flag(c0, c1)
            plan.append((sb.ap[0:1, c0:c1], rhs, st))

        def fn(eng, plan=plan, oc=ones_col, ocb=ones_colb):
            inst = None
            for (outap, rhs, st) in plan:
                ones = ocb if rhs.dtype == BF16 else oc
                inst = eng.matmul(outap, ones[:, :], rhs, start=st, stop=True,
                                  skip_group_check=True)
            return inst
        P.op("tensor", fn, waits, inc=s_t)
        return (s_t, s_t.n)

    def V(fn, waits=(), inc=True):
        w = list(waits) + [(s_v, s_v.n)]
        P.op("vector", fn, w, inc=s_v if inc else None)
        return (s_v, s_v.n)

    def S(fn, waits=(), inc=True):
        w = list(waits) + [(s_s, s_s.n)]
        P.op("scalar", fn, w, inc=s_s if inc else None)
        return (s_s, s_s.n)

    def ln_rows(col0, col1, mm_tok, denom):
        c0, c1 = col0, col1
        V(lambda e: e.tensor_scalar(r_mean[:, c0:c1], S0[0:1, c0:c1],
                                    1.0 / denom, None, ALU.mult), [mm_tok])
        V(lambda e: e.tensor_scalar(r_tmp[:, c0:c1], S1[0:1, c0:c1],
                                    1.0 / denom, None, ALU.mult))
        stats_free = (s_v, s_v.n)
        V(lambda e: e.tensor_tensor(r_aux[:, c0:c1], r_mean[:, c0:c1],
                                    r_mean[:, c0:c1], ALU.mult))
        V(lambda e: e.scalar_tensor_tensor(r_mb[:, c0:c1], r_tmp[:, c0:c1],
                                           EPS, r_aux[:, c0:c1], ALU.add,
                                           ALU.subtract))
        V(lambda e: e.reciprocal(r_aux[:, c0:c1], r_mb[:, c0:c1]))
        tv = (s_v, s_v.n)
        ts = S(lambda e: e.activation(r_tmp[:, c0:c1], r_aux[:, c0:c1],
                                      AF.Sqrt), [tv])
        tmb = V(lambda e: e.tensor_tensor(r_mb[:, c0:c1], r_mean[:, c0:c1],
                                          r_tmp[:, c0:c1], ALU.mult), [ts])
        toks = []
        for cc in range(c0, c1, 512):
            bidx = (cc // 512) % 2
            t1 = mm_group(Pb[2 + bidx][:, :], [(ones_row[:, :], r_tmp[:, cc:cc + 512])],
                          [tmb], 2 + bidx)
            tv1 = V(lambda e, cc=cc, b=bidx: e.tensor_copy(bc_r[:, cc:cc + 512],
                                                           Pb[2 + b][:, :]), [t1])
            bank_free[2 + bidx] = [tv1]
            t2 = mm_group(Pb[2 + bidx][:, :], [(neg_row[:, :], r_mb[:, cc:cc + 512])],
                          [tmb], 2 + bidx)
            tv2 = V(lambda e, cc=cc, b=bidx: e.tensor_copy(bc_mb[:, cc:cc + 512],
                                                           Pb[2 + b][:, :]), [t2])
            bank_free[2 + bidx] = [tv2]
            toks.append(tv2)
        return toks[-1], stats_free

    def ln_cols(dst_base, mm_tok, denom):
        c0, c1 = 0, T_CORE
        V(lambda e: e.tensor_scalar(r_mean[:, c0:c1], S0[0:1, c0:c1],
                                    1.0 / denom, None, ALU.mult), [mm_tok])
        V(lambda e: e.tensor_scalar(r_tmp[:, c0:c1], S1[0:1, c0:c1],
                                    1.0 / denom, None, ALU.mult))
        V(lambda e: e.tensor_tensor(r_aux[:, c0:c1], r_mean[:, c0:c1],
                                    r_mean[:, c0:c1], ALU.mult))
        V(lambda e: e.scalar_tensor_tensor(r_mb[:, c0:c1], r_tmp[:, c0:c1],
                                           EPS, r_aux[:, c0:c1], ALU.add,
                                           ALU.subtract))
        V(lambda e: e.reciprocal(r_aux[:, c0:c1], r_mb[:, c0:c1]))
        tv = (s_v, s_v.n)
        ts = S(lambda e: e.activation(r_tmp[:, c0:c1], r_aux[:, c0:c1],
                                      AF.Sqrt), [tv])
        tmb = V(lambda e: e.scalar_tensor_tensor(
            r_mb[:, c0:c1], r_mean[:, c0:c1], -1.0, r_tmp[:, c0:c1],
            ALU.mult, ALU.mult), [ts])
        def tpc(eng):
            inst = None
            for tt in range(8):
                inst = eng.transpose(Pb[2][:, tt:tt + 1],
                                     r_tmp[0:1, tt * 128:(tt + 1) * 128],
                                     ident[0:1, 0:1])
                inst = eng.transpose(Pb[2][:, 8 + tt:9 + tt],
                                     r_mb[0:1, tt * 128:(tt + 1) * 128],
                                     ident[0:1, 0:1])
            return inst
        P.op("tensor", tpc, [tmb] + bank_free[2], inc=s_t)
        t_c = (s_t, s_t.n)
        evc = V(lambda e, dst_base=dst_base: e.tensor_copy(
            lncol[:, dst_base:dst_base + 16], Pb[2][:, 0:16]), [t_c])
        bank_free[2] = [evc]
        return evc

    def xl(hc, n=None):
        if n is None:
            return XL[:, hc * T_CORE:(hc + 1) * T_CORE]
        return XL[:, hc * T_CORE + n * NT: hc * T_CORE + (n + 1) * NT]

    import contextlib
    stack = contextlib.ExitStack()
    w_stack = contextlib.ExitStack()

    _ph = {}

    def ph_open():
        es = contextlib.ExitStack()
        _ph["es"] = es
        return es

    def sbr(name, shape, dtype=F32):
        return _ph["es"].enter_context(
            nc.sbuf_tensor(name, shape, dtype, side="right"))

    def ph_close():
        _ph["es"].close()

    W_glob = w_stack.enter_context(
        nc.sbuf_tensor("W_glob", [128, 16384], BF16, side="right"))
    p_stack = contextlib.ExitStack()
    asb8 = p_stack.enter_context(
        nc.sbuf_tensor("asb8", [128, HC * 128], BF16, side="right"))
    wr8 = p_stack.enter_context(
        nc.sbuf_tensor("wr8", [128, HC * 8], BF16, side="right"))
    bfl8 = p_stack.enter_context(
        nc.sbuf_tensor("bfl8", [128, D], BF16, side="right"))

    P.dma("sync", ident[:, :], ident_d.ap(), [], s_k)
    P.dma("sync", identb[:, :], identb_d.ap(), [], s_k)
    P.dma("sync", ones_row[:, :], ones_row_d.ap(), [], s_k)
    P.dma("sync", neg_row[:, :], neg_row_d.ap(), [], s_k)
    P.dma("sync", ones_col[:, :], ones_col_d.ap(), [], s_k)
    P.dma("sync", ones_colb[:, :], ones_colb_d.ap(), [], s_k)
    P.dma("sync", esel_sb[:, :], esel_d.ap(), [], s_k)
    P.dma("sync", lp_sb[:, :], lp_pc, [], s_k)
    P.dma("sync", bb_sb[:, :], bb_pc, [], s_k)
    ndf_raw = nc.alloc_sbuf_tensor("ndf_raw", [128, 6], F32, side="left")
    P.dma("sync", ndf_raw[:, :], df_bc, [], s_k)
    tload = (s_k, s_k.n)
    V(lambda e: e.tensor_scalar(ndf_sb[:, :], ndf_raw[:, :], -1.0, None,
                                ALU.mult), [tload])
    P.barrier(s_bar, DMA_SEMS, {"tensor": s_t, "vector": s_v, "scalar": s_s})

    ph_open()
    xraw = sbr("xraw", [128, 2 * D], F32)
    junk1 = sbr("junk1", [128, D], F32)
    stc = sbr("stc", [128, 20], F32)
    tn1 = sbr("tn1", [128, 512], F32)
    xraw_free = [[], []]
    stc_free = [[], []]
    bc_free = [[], []]
    for tt in range(8):
        b = tt % 2
        o = 9 * b
        P.dma("sync", xraw[:, b * D:(b + 1) * D],
              x_in[tt * 128:(tt + 1) * 128, :], xraw_free[b], s_i[b])
        ld = (s_i[b], s_i[b].n)
        tsum = V(lambda e, b=b, o=o: e.reduce_sum(
            stc[:, o + 0:o + 1], xraw[:, b * D:(b + 1) * D], mybir.AxisListType.X),
            [ld])
        tsq = S(lambda e, b=b, o=o: e.activation(
            junk1[:, :], xraw[:, b * D:(b + 1) * D], AF.Square,
            accum_out=stc[:, o + 1:o + 2]), [ld] + stc_free[b])
        V(lambda e, o=o: e.tensor_scalar(stc[:, o + 2:o + 3], stc[:, o + 0:o + 1], 1.0 / D, None,
                                    ALU.mult), [tsum])
        V(lambda e, o=o: e.tensor_scalar(stc[:, o + 3:o + 4], stc[:, o + 1:o + 2], 1.0 / D, None,
                                    ALU.mult), [tsq])
        V(lambda e, o=o: e.tensor_tensor(stc[:, o + 4:o + 5], stc[:, o + 2:o + 3], stc[:, o + 2:o + 3],
                                    ALU.mult))
        V(lambda e, o=o: e.scalar_tensor_tensor(stc[:, o + 5:o + 6], stc[:, o + 3:o + 4], EPS,
                                           stc[:, o + 4:o + 5], ALU.add, ALU.subtract))
        tv = V(lambda e, o=o: e.reciprocal(stc[:, o + 6:o + 7], stc[:, o + 5:o + 6]))
        ts = S(lambda e, o=o: e.activation(stc[:, o + 7:o + 8], stc[:, o + 6:o + 7], AF.Sqrt), [tv])
        tmb = V(lambda e, o=o: e.tensor_tensor(stc[:, o + 8:o + 9], stc[:, o + 2:o + 3],
                                          stc[:, o + 7:o + 8], ALU.mult), [ts])
        def tpr(eng, o=o):
            eng.transpose(Pb[2][0:1, 0:128], stc[:, o + 7:o + 8], ident[:, :])
            return eng.transpose(Pb[2][0:1, 128:256], stc[:, o + 8:o + 9], ident[:, :])
        P.op("tensor", tpr, [tmb] + bank_free[2], inc=s_t)
        t_rows = (s_t, s_t.n)
        ev_rows = V(lambda e: e.tensor_copy(r_tmp[0:1, 0:256], Pb[2][0:1, 0:256]),
                    [t_rows])
        bank_free[2] = [ev_rows]
        def bcast(eng):
            eng.matmul(Pb[3][:, 0:128], ones_row[:, :], r_tmp[0:1, 0:128],
                       start=True, stop=True, skip_group_check=True)
            return eng.matmul(Pb[3][:, 128:256], neg_row[:, :],
                              r_tmp[0:1, 128:256], start=True, stop=True,
                              skip_group_check=True)
        P.op("tensor", bcast, [ev_rows] + bank_free[3] + bc_free[b], inc=s_t)
        t_bc = (s_t, s_t.n)
        ev_bc = V(lambda e, b=b: e.tensor_copy(
            bc_t[:, b * 256:b * 256 + 128], Pb[3][:, 0:128]), [t_bc])
        ev_bc2 = V(lambda e, b=b: e.tensor_copy(
            bc_t[:, b * 256 + 128:b * 256 + 256], Pb[3][:, 128:256]))
        bank_free[3] = [ev_bc2]
        last_tp = None
        for g in range(8):
            pbi = g % 2
            def tp_fn(eng, b=b, g=g, pbi=pbi):
                inst = None
                for jj in range(4):
                    inst = eng.transpose(Pb[pbi][:, jj * 128:(jj + 1) * 128],
                                         xraw[:, b * D + (4 * g + jj) * 128:
                                              b * D + (4 * g + jj + 1) * 128],
                                         ident[:, :])
                return inst
            P.op("tensor", tp_fn, [ld, tsq] + bank_free[pbi], inc=s_t)
            tp = (s_t, s_t.n)
            last_tp = tp
            rrep = bc_t[:, b * 256:b * 256 + 128].unsqueeze(1).broadcast_to(
                [128, 4, 128])
            mrep = bc_t[:, b * 256 + 128:b * 256 + 256].unsqueeze(1).broadcast_to(
                [128, 4, 128])
            t1 = V(lambda e, pbi=pbi, rrep=rrep: e.tensor_tensor(
                tn1[:, :].rearrange("p (c t) -> p c t", c=4),
                Pb[pbi][:, :].rearrange("p (c t) -> p c t", c=4), rrep,
                ALU.mult), [tp, ev_bc2])
            bank_free[pbi] = [t1]
            n = tt // 4
            j = tt % 4
            dst = XL[:, :].rearrange("p (c t) -> p c t", c=HC)[
                :, 4 * g:4 * g + 4, n * NT + j * 128: n * NT + (j + 1) * 128]
            t_xl = V(lambda e, dst=dst, mrep=mrep: e.tensor_tensor(
                dst, tn1[:, :].rearrange("p (c t) -> p c t", c=4), mrep,
                ALU.add))
        xraw_free[b] = [last_tp, tsq, tsum]
        bc_free[b] = [t_xl]
        stc_free[b] = [tmb]
        if tt == 3:
            xl_ready_h0 = [t_xl]
        if tt == 7:
            xl_ready_h1 = [t_xl]
    xl_ready = [xl_ready_h0, xl_ready_h1]

    ph_close()
    if "xl1" in debug_dump:
        dbg_xl = nc.dram_tensor("dbg_xl1", [128, HC * T_CORE], BF16,
                                kind="ExternalOutput").ap()
        P.dma("sync", dbg_xl, XL[:, :], xl_ready[0] + xl_ready[1], s_k)
        P.barrier(s_bar, DMA_SEMS, {"tensor": s_t, "vector": s_v, "scalar": s_s})

    ph_open()
    WB2 = W_glob
    tmpr = sbr("tmpr", [128, 2 * 7 * T_CORE], BF16)
    au_sb = sbr("au_sb", [128, 512], F32)
    aud_sb = sbr("aud_sb", [128, 512], F32)
    delta_sb = sbr("delta_sb", [128, 512], F32)
    absd_sb = sbr("absd_sb", [128, 512], F32)
    logd_sb = sbr("logd_sb", [128, 512], F32)
    keep_sb = sbr("keep_sb", [128, 512], F32)
    dt_sb = sbr("dt_sb", [128, 512], F32)
    dtk_sb = sbr("dtk_sb", [128, 512], F32)
    wb_free = [[], []]
    tmpr_free = [[], []]
    scratch_free = []
    for s in range(SC):
        wb = s % 2
        P.dma("gpsimd", WB2[:, wb * 8192: wb * 8192 + 4096], w_up_p[s],
              wb_free[wb], s_w[wb])
        P.dma("gpsimd", WB2[:, wb * 8192 + 4096: wb * 8192 + 8192],
              w_gate_p[s], [], s_w[wb])
        wtok = (s_w[wb], s_w[wb].n)
        mm_toks = []
        for n in range(NHALF):
            pairs_u = [(WB2[:, wb * 8192 + hc * 128: wb * 8192 + (hc + 1) * 128],
                        xl(hc, n)) for hc in range(HC)]
            t_au = mm_group(Pb[n][:, :], pairs_u, [wtok] + xl_ready[n], n)
            pairs_g = [(WB2[:, wb * 8192 + 4096 + hc * 128:
                            wb * 8192 + 4096 + (hc + 1) * 128],
                        xl(hc, n)) for hc in range(HC)]
            t_gp = mm_group(Pb[2 + n][:, :], pairs_g, [wtok] + xl_ready[n],
                            2 + n)
            mm_toks.append((t_au, t_gp))
        wb_free[wb] = [mm_toks[1][0], mm_toks[1][1]]
        for n in range(NHALF):
            t_au, t_gp = mm_toks[n]
            col = lambda m, wb=wb, n=n: wb * 7 * T_CORE + m * T_CORE + n * NT
            tv_au = V(lambda e, n=n: e.tensor_copy(au_sb[:, :], Pb[n][:, :]),
                      [t_au] + scratch_free + tmpr_free[wb])
            tv_aubf = V(lambda e, c=col(0), n=n: e.tensor_copy(
                tmpr[:, c:c + NT], Pb[n][:, :]))
            tv_delta = V(lambda e, s=s, n=n: e.tensor_scalar(
                delta_sb[:, :], Pb[2 + n][:, :], lp_sb[:, s:s + 1], None,
                ALU.subtract), [t_gp])
            bank_free[n] = [tv_aubf]
            bank_free[2 + n] = [tv_delta]
            ts_abs = S(lambda e: e.activation(absd_sb[:, :], delta_sb[:, :],
                                              AF.Abs), [tv_delta])
            ts_log = S(lambda e: e.activation(logd_sb[:, :], absd_sb[:, :],
                                              AF.Ln))
            tv_aud = V(lambda e: e.tensor_tensor(aud_sb[:, :], delta_sb[:, :],
                                                 au_sb[:, :], ALU.mult))
            tv_keep = V(lambda e: e.tensor_scalar(keep_sb[:, :], absd_sb[:, :],
                                                  THD, None, ALU.is_le),
                        [ts_abs])
            last_v = tv_keep
            for m in range(1, 7):
                odd = (m % 2 == 1)
                scl = float(m - 1) if odd else float(m)
                ts_dt = S(lambda e, scl=scl, m=m: e.activation(
                    dt_sb[:, :], logd_sb[:, :], AF.Exp,
                    bias=ndf_sb[:, m - 1:m], scale=scl),
                    [ts_log, last_v])
                src_dt = dt_sb
                tok_dt = ts_dt
                if m >= 5:
                    tok_dt = V(lambda e: e.tensor_tensor(
                        dtk_sb[:, :], dt_sb[:, :], keep_sb[:, :], ALU.mult),
                        [ts_dt, tv_keep])
                    src_dt = dtk_sb
                other = aud_sb if odd else au_sb
                last_v = V(lambda e, c=col(m), sd=src_dt, o=other:
                           e.tensor_tensor(tmpr[:, c:c + NT], sd[:, :],
                                           o[:, :], ALU.mult),
                           [tok_dt, tv_aud])
            scratch_free = [last_v]
        dst = sc_tmp[:, s].rearrange("m p t -> p m t")
        src = tmpr[:, wb * 7 * T_CORE:(wb + 1) * 7 * T_CORE].rearrange(
            "p (m t) -> p m t", m=7)
        P.dma("sync", dst, src, [scratch_free[0]], s_o[wb])
        tmpr_free[wb] = [(s_o[wb], s_o[wb].n)]
    tmp_spill_done = [(s_o[0], s_o[0].n), (s_o[1], s_o[1].n)]
    wbf_carry = wb_free[0] + wb_free[1]
    P.barrier(s_bar, DMA_SEMS, {"tensor": s_t, "vector": s_v, "scalar": s_s})

    ph_close()
    BIG = stack.enter_context(
        nc.sbuf_tensor("BIG", [128, 28 * T_CORE], BF16, side="right"))

    ph_open()
    WB3 = W_glob
    silu_sb = sbr("silu_sb", [128, 2 * 512], F32)
    a1in = sbr("a1in", [128, T_CORE], F32)
    a1out = sbr("a1out", [128, 2 * T_CORE], F32)
    part_in = [None, sc_a1A, sc_a1B]
    part_out = [sc_a1A, sc_a1B, sc_a1]
    silu_free = [[], [], [], []]
    a1in_free = []
    a1out_free = [[], []]
    a1_spill_tok = {}
    prev_down_tok = []
    for part in range(3):
        fbase = part * FC_P
        wb_free3 = [list(wbf_carry), list(wbf_carry)]
        big_toks = []
        for f in range(FC_P):
            wb = f % 2
            fa = fbase + f
            P.dma("gpsimd", WB3[:, wb * 8192: wb * 8192 + 4096],
                  w_fg_p[fa], wb_free3[wb], s_w[wb])
            P.dma("gpsimd", WB3[:, wb * 8192 + 4096: wb * 8192 + 8192],
                  w_fu_p[fa], [], s_w[wb])
            wtok = (s_w[wb], s_w[wb].n)
            mm_t = []
            for n in range(NHALF):
                pg = [(WB3[:, wb * 8192 + hc * 128: wb * 8192 + (hc + 1) * 128],
                       xl(hc, n)) for hc in range(HC)]
                tg = mm_group(Pb[n][:, :], pg, [wtok], n)
                pu = [(WB3[:, wb * 8192 + 4096 + hc * 128:
                           wb * 8192 + 4096 + (hc + 1) * 128],
                       xl(hc, n)) for hc in range(HC)]
                tu = mm_group(Pb[2 + n][:, :], pu, [wtok], 2 + n)
                mm_t.append((tg, tu))
            wb_free3[wb] = [mm_t[1][0], mm_t[1][1]]
            for n in range(NHALF):
                tg, tu = mm_t[n]
                sb = n
                ts_si = S(lambda e, sb=sb, n=n: e.activation(
                    silu_sb[:, sb * 512:(sb + 1) * 512], Pb[n][:, :],
                    AF.Silu), [tg] + silu_free[sb])
                bank_free[n] = [ts_si]
                col = f * T_CORE + n * NT
                tv = V(lambda e, c=col, sb=sb, n=n: e.tensor_tensor(
                    BIG[:, c:c + NT], silu_sb[:, sb * 512:(sb + 1) * 512],
                    Pb[2 + n][:, :], ALU.mult),
                    [ts_si, tu] + prev_down_tok)
                bank_free[2 + n] = [tv]
                silu_free[sb] = [tv]
                big_toks.append(tv)
        big_ready = big_toks[-1]
        wb_free4 = [[big_ready], [big_ready]]
        for h in range(HC):
            wb = h % 2
            P.dma("gpsimd", WB3[:, wb * 8192: wb * 8192 + FC_P * 128],
                  w_fd_p[h, part], wb_free4[wb], s_w[wb])
            wtok = (s_w[wb], s_w[wb].n)
            if part > 0:
                P.dma("sync", a1in[:, :],
                      part_in[part][h], a1in_free + [a1_spill_tok[h]],
                      s_i[wb])
                a1tok = (s_i[wb], s_i[wb].n)
            ev_toks = []
            for n in range(NHALF):
                pbi = 2 * wb + n
                pairs = [(WB3[:, wb * 8192 + fc * 128: wb * 8192 + (fc + 1) * 128],
                          BIG[:, fc * T_CORE + n * NT: fc * T_CORE + (n + 1) * NT])
                         for fc in range(FC_P)]
                tm = mm_group(Pb[pbi][:, :], pairs,
                              [wtok, big_ready], pbi)
                if part == 0:
                    tv = V(lambda e, pbi=pbi, wb=wb, n=n: e.tensor_copy(
                        a1out[:, wb * T_CORE + n * NT: wb * T_CORE + (n + 1) * NT],
                        Pb[pbi][:, :]), [tm] + a1out_free[wb])
                else:
                    tv = V(lambda e, pbi=pbi, wb=wb, n=n: e.tensor_tensor(
                        a1out[:, wb * T_CORE + n * NT: wb * T_CORE + (n + 1) * NT],
                        Pb[pbi][:, :],
                        a1in[:, n * NT:(n + 1) * NT],
                        ALU.add), [tm, a1tok] + a1out_free[wb])
                bank_free[pbi] = [tv]
                ev_toks.append(tv)
            wb_free4[wb] = [ev_toks[-1]]
            if part > 0:
                a1in_free = [ev_toks[-1]]
            P.dma("sync", part_out[part][h],
                  a1out[:, wb * T_CORE:(wb + 1) * T_CORE], ev_toks, s_o[wb])
            a1out_free[wb] = [(s_o[wb], s_o[wb].n)]
            a1_spill_tok[h] = (s_o[wb], s_o[wb].n)
        wbf_carry = wb_free4[0] + wb_free4[1]
        prev_down_tok = [tm]
    P.barrier(s_bar, DMA_SEMS, {"tensor": s_t, "vector": s_v, "scalar": s_s})

    ph_close()
    ph_open()
    WB5 = W_glob
    a1in5 = sbr("a1in5", [128, T_CORE], F32)
    a_ring = sbr("a_ring", [128, 2 * T_CORE], F32)
    sq5 = sbr("sq5", [128, 2 * 512], BF16)
    ab5 = sbr("ab5", [128, 2 * 512], BF16)
    rtm5 = W_glob[:, 8192:10240].bitcast(F32)
    sc_tmp_flat = sc_tmp.rearrange("m s p t -> (m s) p t")
    pass_in5 = [sc_a1, sc_a1A]
    pass_out5 = [sc_a1A, sc_a]
    last_stats = None
    a1in_free5 = []
    rtm_free = []
    a_free = [[], []]
    sq_free5 = [[], []]
    pending5 = [None, None]
    stats_guard5 = [[], []]
    for qp in range(2):
        qbase = qp * 28
        prev_mm_tok = [] if qp == 0 else [big5_last_mm]
        for blk in range(4):
            s0_ = qbase + blk * 7
            src = sc_tmp_flat[s0_:s0_ + 7].rearrange("c p t -> p c t")
            dst = BIG[:, blk * 7 * T_CORE:(blk + 1) * 7 * T_CORE].rearrange(
                "p (c t) -> p c t", c=7)
            P.dma("sync", dst, src,
                  (tmp_spill_done if (qp == 0 and blk == 0) else []) +
                  prev_mm_tok, s_j[blk % 2])
        tmp_rd = (s_j[0], s_j[0].n)
        tmp_rd2 = (s_j[1], s_j[1].n)
        wb_free5 = [list(wbf_carry), list(wbf_carry)]
        for h in range(HC):
            wb = h % 2
            src_w = w_cat_p[h][:, qbase * 128:(qbase + 28) * 128]
            P.dma("gpsimd", WB5[:, wb * 3584:(wb + 1) * 3584], src_w,
                  wb_free5[wb], s_w[wb])
            wtok = (s_w[wb], s_w[wb].n)
            P.dma("sync", a1in5[:, :],
                  pass_in5[qp][h], a1in_free5 + [a1_spill_tok[h]], s_i[wb])
            a1tok = (s_i[wb], s_i[wb].n)
            ev_toks = []
            for n in range(NHALF):
                pbi = 2 * wb + n
                pairs = [(WB5[:, wb * 3584 + q * 128: wb * 3584 + (q + 1) * 128],
                          BIG[:, q * T_CORE + n * NT: q * T_CORE + (n + 1) * NT])
                         for q in range(28)]
                tm = mm_group(Pb[pbi][:, :], pairs, [wtok, tmp_rd, tmp_rd2], pbi)
                big5_last_mm = tm
                if qp == 1 and pending5[n] is not None:
                    entries, swaits, pwb = pending5[n]
                    last_stats = stats_mms(entries, swaits)
                    sq_free5[n] = [last_stats]
                    stats_guard5[pwb] = [last_stats]
                    pending5[n] = None
                tv = V(lambda e, pbi=pbi, wb=wb, n=n: e.tensor_tensor(
                    a_ring[:, wb * T_CORE + n * NT: wb * T_CORE + (n + 1) * NT],
                    Pb[pbi][:, :],
                    a1in5[:, n * NT:(n + 1) * NT],
                    ALU.add), [tm, a1tok] + a_free[wb] + stats_guard5[wb])
                bank_free[pbi] = [tv]
                if qp == 1:
                    tab = V(lambda e, wb=wb, n=n: e.tensor_copy(
                        ab5[:, n * 512:(n + 1) * 512],
                        a_ring[:, wb * T_CORE + n * NT: wb * T_CORE + (n + 1) * NT]),
                        [tv] + sq_free5[n])
                    tsq = S(lambda e, wb=wb, n=n: e.activation(
                        sq5[:, n * 512:(n + 1) * 512],
                        a_ring[:, wb * T_CORE + n * NT: wb * T_CORE + (n + 1) * NT],
                        AF.Square), [tv] + sq_free5[n])
                    pending5[n] = (
                        [(st0, n * NT, (n + 1) * NT,
                          ab5[:, n * 512:(n + 1) * 512]),
                         (st1, n * NT, (n + 1) * NT,
                          sq5[:, n * 512:(n + 1) * 512])], [tab, tsq], wb)
                ev_toks.append(tv)
            wb_free5[wb] = [ev_toks[-1]]
            a1in_free5 = [ev_toks[-1]]
            if qp == 0:
                P.dma("sync", pass_out5[qp][h],
                      a_ring[:, wb * T_CORE:(wb + 1) * T_CORE], ev_toks,
                      s_o[wb])
                a_free[wb] = [(s_o[wb], s_o[wb].n)]
                a1_spill_tok[h] = (s_o[wb], s_o[wb].n)
            else:
                tm_evs = []
                for rnd in range(2):
                    pbt = 2 * wb + rnd
                    def tpa(eng, wb=wb, rnd=rnd, pbt=pbt):
                        inst = None
                        for piece in range(4):
                            tt_ = 4 * rnd + piece
                            inst = eng.transpose(
                                Pb[pbt][:, piece * 128:(piece + 1) * 128],
                                a_ring[:, wb * T_CORE + tt_ * 128:
                                       wb * T_CORE + (tt_ + 1) * 128],
                                ident[:, :])
                        return inst
                    P.op("tensor", tpa, ev_toks + bank_free[pbt] + rtm_free,
                         inc=s_t)
                    t_tp = (s_t, s_t.n)
                    tm_ev = V(lambda e, rnd=rnd, pbt=pbt: e.tensor_copy(
                        rtm5[:, rnd * 512:(rnd + 1) * 512], Pb[pbt][:, :]),
                        [t_tp])
                    bank_free[pbt] = [tm_ev]
                    tm_evs.append(tm_ev)
                dst_tm = sc_a[:, :, h * 128:(h + 1) * 128].rearrange(
                    "c p t -> p c t")
                P.dma("sync", dst_tm,
                      rtm5[:, :].rearrange("p (c t) -> p c t", c=8),
                      tm_evs, s_o[wb])
                rtm_free = [(s_o[wb], s_o[wb].n)]
                a_free[wb] = tm_evs + [(s_o[wb], s_o[wb].n)]
        if qp == 1:
            for n in range(NHALF):
                if pending5[n] is not None:
                    entries, swaits, pwb = pending5[n]
                    last_stats = stats_mms(entries, swaits)
                    pending5[n] = None
        wbf_carry = wb_free5[0] + wb_free5[1]
    lna_tok = ln_cols(0, last_stats, D)
    st0.reset()
    st1.reset()
    P.barrier(s_bar, DMA_SEMS, {"tensor": s_t, "vector": s_v, "scalar": s_s})

    ph_close()
    stack.close()

    ph_open()
    xraw6 = sbr("xraw6", [128, D], F32)
    a_fm = sbr("a_fm", [128, D], F32)
    ev6r = sbr("ev6r", [128, 2 * 512], F32)
    x2t = sbr("x2t", [128, 2 * D], F32)
    xlnb6 = sbr("xlnb6", [128, 2048], BF16)
    stc6 = sbr("stc6", [128, 20], F32)
    x_free6 = []
    a_free6 = []
    atm_free = [[], []]
    x2t_free = [[], []]
    xlnb_free = []
    stc6_free = [[], []]
    for tt in range(8):
        b = tt % 2
        o = 9 * b
        P.dma("sync", a_fm[:, :], sc_a[tt],
              a_free6, s_j[b])
        lda = (s_j[b], s_j[b].n)
        P.dma("sync", xraw6[:, :], x_in[tt * 128:(tt + 1) * 128, :],
              x_free6, s_i[0])
        ldx = (s_i[0], s_i[0].n)
        ev_toks6 = []
        for g in range(8):
            ev = S(lambda e, b=b, g=g, tt=tt: e.activation(
                ev6r[:, (g % 2) * 512:(g % 2 + 1) * 512],
                a_fm[:, g * 512:(g + 1) * 512],
                AF.Identity, bias=lncol[:, 8 + tt:9 + tt],
                scale=lncol[:, tt:tt + 1]),
                [lda, lna_tok] + atm_free[g % 2])
            tadd = V(lambda e, b=b, g=g: e.tensor_tensor(
                x2t[:, b * D + g * 512: b * D + (g + 1) * 512],
                ev6r[:, (g % 2) * 512:(g % 2 + 1) * 512],
                xraw6[:, g * 512:(g + 1) * 512], ALU.add),
                [ev, ldx] + x2t_free[b])
            atm_free[g % 2] = [tadd]
            ev_toks6.append(tadd)
        t_x2 = ev_toks6[-1]
        tsum = V(lambda e, b=b, o=o: e.reduce_sum(
            stc6[:, o:o + 1], x2t[:, b * D:(b + 1) * D],
            mybir.AxisListType.X), stc6_free[b])
        tsq = S(lambda e, b=b, o=o: e.activation(
            xraw6[:, :], x2t[:, b * D:(b + 1) * D], AF.Square,
            accum_out=stc6[:, o + 1:o + 2]), [t_x2])
        a_free6 = [ev_toks6[-1]]
        x_free6 = [t_x2, tsq]
        V(lambda e, o=o: e.tensor_scalar(stc6[:, o + 2:o + 3], stc6[:, o:o + 1],
                                         1.0 / D, None, ALU.mult), [tsum, tsq])
        V(lambda e, o=o: e.tensor_scalar(stc6[:, o + 3:o + 4],
                                         stc6[:, o + 1:o + 2], 1.0 / D, None,
                                         ALU.mult))
        V(lambda e, o=o: e.tensor_tensor(stc6[:, o + 4:o + 5],
                                         stc6[:, o + 2:o + 3],
                                         stc6[:, o + 2:o + 3], ALU.mult))
        V(lambda e, o=o: e.scalar_tensor_tensor(
            stc6[:, o + 5:o + 6], stc6[:, o + 3:o + 4], EPS,
            stc6[:, o + 4:o + 5], ALU.add, ALU.subtract))
        tv6 = V(lambda e, o=o: e.reciprocal(stc6[:, o + 6:o + 7],
                                            stc6[:, o + 5:o + 6]))
        ts6 = S(lambda e, o=o, tt=tt: e.activation(
            lncol[:, 16 + tt:17 + tt], stc6[:, o + 6:o + 7], AF.Sqrt), [tv6])
        tmb6 = V(lambda e, o=o, tt=tt: e.scalar_tensor_tensor(
            lncol[:, 24 + tt:25 + tt], stc6[:, o + 2:o + 3], -1.0,
            lncol[:, 16 + tt:17 + tt], ALU.mult, ALU.mult), [ts6])
        stc6_free[b] = [tmb6]
        spill_toks = []
        last_tpx = None
        for hh in range(2):
            t_xlb = V(lambda e, b=b, hh=hh, tt=tt: e.tensor_scalar(
                xlnb6[:, :], x2t[:, b * D + hh * 2048: b * D + (hh + 1) * 2048],
                lncol[:, 16 + tt:17 + tt],
                lncol[:, 24 + tt:25 + tt], ALU.mult, ALU.add),
                [tmb6] + xlnb_free)
            for g in range(4 * hh, 4 * hh + 4):
                pbi = 2 + g % 2
                def tpx(eng, g=g, hh=hh, pbi=pbi):
                    inst = None
                    pbb = Pb[pbi][:, :].bitcast(BF16)
                    for jj in range(4):
                        c_loc = (4 * g + jj) - hh * 16
                        inst = eng.transpose(pbb[:, jj * 128:(jj + 1) * 128],
                                             xlnb6[:, c_loc * 128:
                                                   (c_loc + 1) * 128],
                                             identb[:, :])
                    return inst
                P.op("tensor", tpx, [t_xlb] + bank_free[pbi], inc=s_t)
                tpt = (s_t, s_t.n)
                last_tpx = tpt
                n_ = tt // 4
                j_ = tt % 4
                dst = XL[:, :].rearrange("p (c t) -> p c t", c=HC)[
                    :, 4 * g:4 * g + 4,
                    n_ * NT + j_ * 128: n_ * NT + (j_ + 1) * 128]
                evx = S(lambda e, dst=dst, pbi=pbi: e.activation(
                    dst, Pb[pbi][:, :].bitcast(BF16)[:, 0:512].rearrange(
                        "p (c t) -> p c t", c=4), AF.Copy), [tpt])
                bank_free[pbi] = [evx]
            xlnb_free = [last_tpx]
        P.dma("sync", sc_x2[tt], x2t[:, b * D:(b + 1) * D],
              [t_x2, tsq, tsum, last_tpx], s_o[b])
        x2t_free[b] = [(s_o[b], s_o[b].n)]
    P.barrier(s_bar, DMA_SEMS, {"tensor": s_t, "vector": s_v, "scalar": s_s})
    ph_close()

    ph_open()
    WB8 = W_glob
    lg_fm = sbr("lg_fm", [8, T_CORE], F32)
    h_sb = sbr("h_sb", [128, T_CORE], F32)
    lg_tm = sbr("lg_tm", [128, 64], F32)
    m1_sb = sbr("m1_sb", [128, 8], F32)
    m2_sb = sbr("m2_sb", [128, 8], F32)
    d1_sb = sbr("d1_sb", [128, 64], F32)
    d2_sb = sbr("d2_sb", [128, 64], F32)
    msk_sb = sbr("msk_sb", [128, 64], F32)
    e2_sb = sbr("e2_sb", [128, 8], F32)
    den_sb = sbr("den_sb", [128, 8], F32)
    w1_sb = sbr("w1_sb", [128, 8], F32)
    w2_sb = sbr("w2_sb", [128, 8], F32)
    wtm_sb = sbr("wtm_sb", [128, 64], F32)
    wfm_sb = sbr("wfm_sb", [8, T_CORE], F32)
    hw_sb = sbr("hw_sb", [128, T_CORE], BF16)
    f_ring = sbr("f_ring", [128, 2 * T_CORE], F32)
    sq8 = sbr("sq8", [128, 2 * 512], BF16)
    fb8 = sbr("fb8", [128, 2 * 512], BF16)
    P.dma("gpsimd", asb8[:, :].rearrange("p (c f) -> p c f", c=HC),
          a_T.rearrange("(c p) f -> p c f", p=128), [], s_k2)
    P.dma("gpsimd", wr8[:, :].rearrange("p (c f) -> p c f", c=HC),
          w_rT.rearrange("(c p) f -> p c f", p=128), [], s_k2)
    P.dma("gpsimd", bfl8[:, :], b_fl, [], s_k2)
    pre = (s_k2, s_k2.n)
    for n in range(NHALF):
        pairs = [(wr8[:, hc * 8:(hc + 1) * 8], xl(hc, n)) for hc in range(HC)]
        tl = mm_group(Pb[n][0:8, :], pairs, [pre], n)
        tvl = V(lambda e, n=n: e.tensor_copy(lg_fm[:, n * NT:(n + 1) * NT],
                                             Pb[n][0:8, :]), [tl])
        bank_free[n] = [tvl]
        pairs_h = [(asb8[:, hc * 128:(hc + 1) * 128], xl(hc, n))
                   for hc in range(HC)]
        th = mm_group(Pb[2 + n][:, :], pairs_h, [pre], 2 + n)
        tvh = V(lambda e, n=n: e.tensor_copy(h_sb[:, n * NT:(n + 1) * NT],
                                             Pb[2 + n][:, :]), [th])
        bank_free[2 + n] = [tvh]
    def tplg(eng):
        inst = None
        for tt in range(8):
            inst = eng.transpose(Pb[0][:, tt * 8:(tt + 1) * 8],
                                 lg_fm[0:8, tt * 128:(tt + 1) * 128],
                                 ident[0:8, 0:8])
        return inst
    P.op("tensor", tplg, [tvl] + bank_free[0], inc=s_t)
    t_lgt = (s_t, s_t.n)
    tv = V(lambda e: e.tensor_copy(lg_tm[:, :], Pb[0][:, 0:64]), [t_lgt])
    bank_free[0] = [tv]
    lg3 = lambda ap: ap.rearrange("p (t e) -> p t e", e=8)
    V(lambda e: e.tensor_reduce(m1_sb[:, :], lg3(lg_tm[:, :]),
                                mybir.AxisListType.X, ALU.max))
    m1b = m1_sb[:, :].unsqueeze(2).broadcast_to([128, 8, 8])
    V(lambda e: e.tensor_tensor(lg3(d1_sb[:, :]), lg3(lg_tm[:, :]), m1b,
                                ALU.is_equal))
    V(lambda e: e.scalar_tensor_tensor(msk_sb[:, :], d1_sb[:, :], -1e30,
                                       lg_tm[:, :], ALU.mult, ALU.add))
    V(lambda e: e.tensor_reduce(m2_sb[:, :], lg3(msk_sb[:, :]),
                                mybir.AxisListType.X, ALU.max))
    tvm = V(lambda e: e.tensor_tensor(den_sb[:, :], m2_sb[:, :],
                                      m1_sb[:, :], ALU.subtract))
    tse = S(lambda e: e.activation(e2_sb[:, :], den_sb[:, :], AF.Exp),
            [tvm])
    V(lambda e: e.tensor_scalar(den_sb[:, :], e2_sb[:, :], 1.0, None,
                                ALU.add), [tse])
    V(lambda e: e.reciprocal(w1_sb[:, :], den_sb[:, :]))
    tvw = V(lambda e: e.tensor_tensor(w2_sb[:, :], w1_sb[:, :], e2_sb[:, :],
                                      ALU.mult))
    m2b = m2_sb[:, :].unsqueeze(2).broadcast_to([128, 8, 8])
    V(lambda e: e.tensor_tensor(lg3(d2_sb[:, :]), lg3(msk_sb[:, :]), m2b,
                                ALU.is_equal))
    w1b = w1_sb[:, :].unsqueeze(2).broadcast_to([128, 8, 8])
    w2b = w2_sb[:, :].unsqueeze(2).broadcast_to([128, 8, 8])
    V(lambda e: e.tensor_tensor(lg3(wtm_sb[:, :]), lg3(d1_sb[:, :]), w1b,
                                ALU.mult), [tvw])
    V(lambda e: e.tensor_tensor(lg3(d2_sb[:, :]), lg3(d2_sb[:, :]), w2b,
                                ALU.mult))
    tv_wtm = V(lambda e: e.tensor_tensor(wtm_sb[:, :], wtm_sb[:, :],
                                         d2_sb[:, :], ALU.add))
    def tpw(eng):
        inst = None
        for tt in range(8):
            dst = Pb[tt // 4][0:8, (tt % 4) * 128:(tt % 4 + 1) * 128]
            inst = eng.transpose(dst, wtm_sb[:, tt * 8:(tt + 1) * 8],
                                 ident[:, :])
        return inst
    P.op("tensor", tpw, [tv_wtm] + bank_free[0] + bank_free[1], inc=s_t)
    t_wfm = (s_t, s_t.n)
    tv_wfm0 = V(lambda e: e.tensor_copy(wfm_sb[0:8, 0:512], Pb[0][0:8, :]),
                [t_wfm])
    tv_wfm = V(lambda e: e.tensor_copy(wfm_sb[0:8, 512:1024], Pb[1][0:8, :]))
    bank_free[0] = [tv_wfm0]
    bank_free[1] = [tv_wfm]
    for n in range(NHALF):
        tsel = mm_group(Pb[2 + n][:, :],
                        [(esel_sb[:, :], wfm_sb[0:8, n * NT:(n + 1) * NT])],
                        [tv_wfm], 2 + n)
        thw = V(lambda e, n=n: e.tensor_tensor(
            hw_sb[:, n * NT:(n + 1) * NT], h_sb[:, n * NT:(n + 1) * NT],
            Pb[2 + n][:, :], ALU.mult), [tsel, tvh])
        bank_free[2 + n] = [thw]
    hw_tok = (s_v, s_v.n)
    rtm8 = W_glob[:, 8192:10240].bitcast(F32)
    rtm_free8 = []
    wb_free8 = [list(wbf_carry), list(wbf_carry)]
    f_free = [[], []]
    sq_free8 = [[], []]
    pending8 = [None, None]
    stats_guard8 = [[], []]
    last_stats8 = None
    for h in range(HC):
        wb = h % 2
        P.dma("gpsimd", WB8[:, wb * 4096:(wb + 1) * 4096], w_b_p[h],
              wb_free8[wb], s_w[wb])
        wtok = (s_w[wb], s_w[wb].n)
        ev_toks = []
        for n in range(NHALF):
            pbi = 2 * wb + n
            pairs = [(WB8[:, wb * 4096 + hc * 128: wb * 4096 + (hc + 1) * 128],
                      xl(hc, n)) for hc in range(HC)]
            pairs.append((bfl8[:, h * 128:(h + 1) * 128],
                          hw_sb[:, n * NT:(n + 1) * NT]))
            tm = mm_group(Pb[pbi][:, :], pairs, [wtok, hw_tok, pre], pbi)
            if pending8[n] is not None:
                entries, swaits, pwb = pending8[n]
                last_stats8 = stats_mms(entries, swaits)
                sq_free8[n] = [last_stats8]
                stats_guard8[pwb] = [last_stats8]
                pending8[n] = None
            tv = V(lambda e, pbi=pbi, wb=wb, n=n, h=h: e.tensor_scalar(
                f_ring[:, wb * T_CORE + n * NT: wb * T_CORE + (n + 1) * NT],
                Pb[pbi][:, :], bb_sb[:, h:h + 1], None, ALU.add),
                [tm] + f_free[wb] + stats_guard8[wb])
            bank_free[pbi] = [tv]
            tfb = V(lambda e, wb=wb, n=n: e.tensor_copy(
                fb8[:, n * 512:(n + 1) * 512],
                f_ring[:, wb * T_CORE + n * NT: wb * T_CORE + (n + 1) * NT]),
                [tv] + sq_free8[n])
            tsq = S(lambda e, wb=wb, n=n: e.activation(
                sq8[:, n * 512:(n + 1) * 512],
                f_ring[:, wb * T_CORE + n * NT: wb * T_CORE + (n + 1) * NT],
                AF.Square), [tv] + sq_free8[n])
            pending8[n] = (
                [(st0, n * NT, (n + 1) * NT,
                  fb8[:, n * 512:(n + 1) * 512]),
                 (st1, n * NT, (n + 1) * NT,
                  sq8[:, n * 512:(n + 1) * 512])], [tfb, tsq], wb)
            ev_toks.append(tv)
        wb_free8[wb] = [ev_toks[-1]]
        tm_evs8 = []
        for rnd in range(2):
            pbt = 2 * wb + rnd
            def tpf(eng, wb=wb, rnd=rnd, pbt=pbt):
                inst = None
                for piece in range(4):
                    tt_ = 4 * rnd + piece
                    inst = eng.transpose(
                        Pb[pbt][:, piece * 128:(piece + 1) * 128],
                        f_ring[:, wb * T_CORE + tt_ * 128:
                               wb * T_CORE + (tt_ + 1) * 128],
                        ident[:, :])
                return inst
            P.op("tensor", tpf, ev_toks + bank_free[pbt] + rtm_free8,
                 inc=s_t)
            t_tp = (s_t, s_t.n)
            tm_ev = V(lambda e, rnd=rnd, pbt=pbt: e.tensor_copy(
                rtm8[:, rnd * 512:(rnd + 1) * 512], Pb[pbt][:, :]), [t_tp])
            bank_free[pbt] = [tm_ev]
            tm_evs8.append(tm_ev)
        dst_tm = sc_f[:, :, h * 128:(h + 1) * 128].rearrange("c p t -> p c t")
        P.dma("sync", dst_tm, rtm8[:, :].rearrange("p (c t) -> p c t", c=8),
              tm_evs8, s_o[wb])
        rtm_free8 = [(s_o[wb], s_o[wb].n)]
        f_free[wb] = tm_evs8 + [(s_o[wb], s_o[wb].n)]
    for n in range(NHALF):
        if pending8[n] is not None:
            entries, swaits, pwb = pending8[n]
            last_stats8 = stats_mms(entries, swaits)
            pending8[n] = None
    ln3_tok = ln_cols(32, last_stats8, D)
    st0.reset()
    st1.reset()
    P.barrier(s_bar, DMA_SEMS, {"tensor": s_t, "vector": s_v, "scalar": s_s})

    ph_close()
    p_stack.close()
    w_stack.close()
    ph_open()
    x2t9 = sbr("x2t9", [128, 2 * D], F32)
    ft9 = sbr("ft9", [128, 2 * D], F32)
    lnf = sbr("lnf", [128, D], F32)
    oring = sbr("oring", [128, 2 * D], F32)
    x2_free9 = [[], []]
    f_free9 = [[], []]
    lnf_free = []
    o_free9 = [[], []]
    for tt in range(8):
        b = tt % 2
        P.dma("sync", x2t9[:, b * D:(b + 1) * D], sc_x2[tt],
              x2_free9[b], s_i[b])
        ldx2 = (s_i[b], s_i[b].n)
        P.dma("sync", ft9[:, b * D:(b + 1) * D], sc_f[tt],
              f_free9[b], s_j[b])
        ldf = (s_j[b], s_j[b].n)
        t_ln = S(lambda e, b=b, tt=tt: e.activation(
            lnf[:, :], ft9[:, b * D:(b + 1) * D],
            AF.Identity, bias=lncol[:, 40 + tt:41 + tt],
            scale=lncol[:, 32 + tt:33 + tt]),
            [ldf, ln3_tok] + lnf_free)
        t_out = V(lambda e, b=b: e.tensor_tensor(
            oring[:, b * D:(b + 1) * D], lnf[:, :],
            x2t9[:, b * D:(b + 1) * D], ALU.add),
            [ldx2, t_ln] + o_free9[b])
        lnf_free = [t_out]
        x2_free9[b] = [t_out]
        f_free9[b] = [t_ln]
        P.dma("sync", out_hbm[tt * 128:(tt + 1) * 128, :],
              oring[:, b * D:(b + 1) * D], [t_out], s_o[b])
        o_free9[b] = [(s_o[b], s_o[b].n)]
    ph_close()
    P.op("sync", lambda e: e.nop(), [(s, s.n) for s in ALL_DMA_SEMS])
    P.op("gpsimd", lambda e: e.nop(), [(s_o[0], s_o[0].n), (s_o[1], s_o[1].n)])

    P.emit(cut=_CUT[0] if _CUT else None)
    return nc



def _prep_host(inputs):
    c = np.ascontiguousarray
    up_w = inputs["L0_attn_up_w"]
    gate_w = inputs["L0_attn_gate_w"]
    lp = inputs["L0_attn_local_point"]
    lao = inputs["L0_attn_local_approx_output"]
    fuse = inputs["L0_attn_fuse_weight"]
    df = inputs["L0_attn_discount_factor"]
    fg = inputs["L0_attn_ffn_gate_w"]
    fu = inputs["L0_attn_ffn_up_w"]
    fd = inputs["L0_attn_ffn_down_w"]
    wb = inputs["L0_ffn_W_base"]
    bb = inputs["L0_ffn_b_base"]
    wr = inputs["L0_ffn_W_router"]
    A = inputs["L0_ffn_A"]
    B = inputs["L0_ffn_B"]
    f32 = np.float32
    def tile_kM(wT, steps):
        kt, mt = wT.shape
        return c(wT.reshape(kt // 128, 128, steps, mt // steps)
                 .transpose(2, 1, 0, 3).reshape(steps, 128, -1))

    w_catT = np.concatenate(
        [np.asarray(lao, f32).T] +
        [np.asarray(fuse[:, :, m], f32).T for m in range(6)], axis=0)
    fdT = np.asarray(fd, f32).T
    shared = {
        "w_up_p": tile_kM(np.asarray(up_w, f32).T, SC),
        "w_gate_p": tile_kM(np.asarray(gate_w, f32).T, SC),
        "lp_pc": c(np.asarray(lp, f32).reshape(SC, 128).T),
        "df_bc": c(np.broadcast_to(np.asarray(df, f32), (128, 6))),
        "w_cat_p": tile_kM(w_catT, HC),
        "w_fg_p": tile_kM(np.asarray(fg, f32).T, 78),
        "w_fu_p": tile_kM(np.asarray(fu, f32).T, 78),
        "w_fd_p": c(fdT.reshape(3, FC_P, 128, HC, 128)
                    .transpose(3, 0, 2, 1, 4).reshape(HC, 3, 128, FC_P * 128)),
        "w_b_p": tile_kM(np.asarray(wb, f32).T, HC),
        "bb_pc": c(np.asarray(bb, f32).reshape(HC, 128).T),
        "w_rT": c(np.asarray(wr, f32).T),
        "a_T": c(np.asarray(A, f32).reshape(128, D).T),
        "b_fl": c(np.asarray(B, f32).transpose(0, 2, 1).reshape(128, D)),
    }
    return shared


def kernel(**inputs):
    x = np.asarray(inputs["L0_attn_hidden_states"], np.float32).reshape(8192, D)
    shared = _prep_host(inputs)
    nc = build_core_program(_DEBUG_DUMP)
    in_maps = []
    for cid in range(8):
        m = dict(shared)
        m["x_in"] = np.ascontiguousarray(x[cid * T_CORE:(cid + 1) * T_CORE])
        in_maps.append(m)
    res = run_bass_kernel_spmd(nc, in_maps, core_ids=list(range(8)))
    out = np.concatenate([res.results[cid]["out"] for cid in range(8)], axis=0)
    if _DEBUG_DUMP:
        kernel._dbg = res.results
    return out.reshape(1, 8192, D)
